# version 15
# baseline (speedup 1.0000x reference)
import math
import contextlib
import numpy as np
import ml_dtypes

import concourse.bass as bass
import concourse.mybir as mybir
from concourse.bass_utils import run_bass_kernel_spmd

F32 = mybir.dt.float32
BF16 = mybir.dt.bfloat16
U8 = mybir.dt.uint8
AF = mybir.ActivationFunctionType
OP = mybir.AluOpType
BF = ml_dtypes.bfloat16

D = 4096
S = 8192
NCORE = 8
OWN = 1024
SLOC = 2048
NDT = 32
SM = 1.0 / math.sqrt(128.0)
EPS = 1e-5



def _tth():
    n = D
    sqrt_n = math.sqrt(n)
    pair_scale = math.sqrt(n / 2.0)
    max_f = (n - 1) // 2
    eye = np.eye(n, dtype=np.float64)
    dc = (eye[:, 0] * sqrt_n)[:, None].astype(np.complex128)
    mid = eye[:, 1:1 + 2 * max_f:2] * pair_scale - 1j * (eye[:, 2:2 + 2 * max_f:2] * pair_scale)
    ny = (eye[:, -1] * sqrt_n)[:, None].astype(np.complex128)
    spectrum = np.concatenate([dc, mid, ny], axis=1)
    T = np.fft.irfft(spectrum, n=n, axis=1).T
    return np.ascontiguousarray((0.5 * T).T).astype(BF)


def _masks8():
    s = np.arange(128)[:, None]
    q = np.arange(512)[None, :]
    out = np.zeros((8, 128, 512), np.float32)
    for j in range(4):
        out[j] = (128 * j + s > q)
    for j in range(8, 12):
        out[j - 4] = (128 * j + s <= q + 1024)
    return np.ascontiguousarray(out.reshape(8 * 128, 512)).astype(BF)


def _rope_tables(core):
    half = 64
    inv = 10000.0 ** (-np.arange(half, dtype=np.float64) / half)
    pos = np.arange(core * OWN - 1024, core * OWN + 1024, dtype=np.float64)
    ang = inv[:, None] * pos[None, :]
    cos, sin = np.cos(ang), np.sin(ang)
    cosd = np.concatenate([cos, cos], axis=0).astype(BF)
    sinsgn = np.concatenate([-sin, sin], axis=0).astype(BF)
    return np.ascontiguousarray(cosd), np.ascontiguousarray(sinsgn)


def _corr(core):
    c = np.zeros(OWN, np.float32)
    if core == 0:
        c = np.maximum(0, 1023 - np.arange(OWN)).astype(np.float32)
    return np.ascontiguousarray(np.broadcast_to(c[None, :], (128, OWN)))



def build_nc(upto=99, taps=(), nunits=64):
    nc = bass.Bass("TRN2", target_bir_lowering=False)

    xT = nc.dram_tensor("xT", [D, SLOC], F32, kind="ExternalInput").ap()
    wq = nc.dram_tensor("wq", [D, D], F32, kind="ExternalInput").ap()
    wk = nc.dram_tensor("wk", [D, 1024], F32, kind="ExternalInput").ap()
    wv = nc.dram_tensor("wv", [D, 1024], F32, kind="ExternalInput").ap()
    wo = nc.dram_tensor("wo", [D, D], F32, kind="ExternalInput").ap()
    wbT = nc.dram_tensor("wbT", [D, D], F32, kind="ExternalInput").ap()
    bbt = nc.dram_tensor("bbt", [128, 32], F32, kind="ExternalInput").ap()
    enc01T = nc.dram_tensor("enc01T", [D, 40], F32, kind="ExternalInput").ap()
    coefs01 = nc.dram_tensor("coefs01", [D, 120], F32, kind="ExternalInput").ap()
    mask01 = nc.dram_tensor("mask01", [D, 120], U8, kind="ExternalInput").ap()
    tth = nc.dram_tensor("tth", [D, D], BF16, kind="ExternalInput").ap()
    ropec = nc.dram_tensor("ropec", [128, SLOC], BF16, kind="ExternalInput").ap()
    ropes = nc.dram_tensor("ropes", [128, SLOC], BF16, kind="ExternalInput").ap()
    masks8 = nc.dram_tensor("masks8", [8 * 128, 512], BF16, kind="ExternalInput").ap()
    corr = nc.dram_tensor("corr", [128, OWN], F32, kind="ExternalInput").ap()
    vsp = nc.dram_tensor("vsp", [SLOC, 1024], BF16).ap()
    gsp = nc.dram_tensor("gsp", [D, OWN], F32).ap()
    out_t = nc.dram_tensor("outT", [D, OWN], F32, kind="ExternalOutput").ap()

    arena = nc.alloc_sbuf_tensor("arena", [128, 210848], U8)
    BASE = nc.lookup_mloc(arena).addr

    def sbat(name, shape, dtype, off):
        return nc.alloc_sbuf_tensor_at(name, shape, dtype, offset=BASE + off)

    XNP = sbat("XNP", [128, 32768], BF16, 0)
    XNO = sbat("XNO", [128, 32768], BF16, 65536)
    KT = sbat("KT", [128, 16384], BF16, 131072)
    COEFC = sbat("COEFC", [128, 3840], BF16, 131072)
    COEFM = sbat("COEFM", [128, 3840], BF16, 138752)
    XBUF = [sbat(f"XBUF{i}", [128, SLOC], F32, 131072 + 8192 * i) for i in range(2)]
    TTB = [sbat(f"TTB{i}", [128, 2048], BF16, 147456 + 4096 * i) for i in range(2)]
    TTS2 = [sbat("TTS2a", [128, 4096], BF16, 147456), sbat("TTS2b", [128, 4096], BF16, 155648),
            sbat("TTS2c", [128, 4096], BF16, 163840)]
    TMPB = sbat("TMPB", [128, SLOC], BF16, 131072)
    MBCB = sbat("MBCB", [128, SLOC], BF16, 139264)
    RBCB = sbat("RBCB", [128, SLOC], BF16, 143360)
    XBUF2 = sbat("XBUF2", [128, SLOC], F32, 147456)
    XSQ = [sbat(f"XSQ{i}", [128, SLOC], BF16, 155648 + 4096 * i) for i in range(2)]
    COSD = sbat("COSD", [128, SLOC], BF16, 163840)
    SINSGN = sbat("SINSGN", [128, SLOC], BF16, 167936)
    TMPF = sbat("TMPF", [128, SLOC], F32, 163840)
    G = [sbat(f"G{i}", [128, OWN], F32, 163840 + 4096 * i) for i in range(2)]
    A01T = sbat("A01T", [40, D], BF16, 172032)
    MASKS = sbat("MASKS", [128, 4096], BF16, 172032)
    WCH = [sbat(f"WCH{i}", [128, 1024], BF16, 180224 + 2048 * i) for i in range(2)]
    PB = sbat("PB", [128, 1024], BF16, 184320)
    TMS = sbat("TMS", [128, 1024], BF16, 186368)
    QROWS = sbat("QROWS", [128, 4096], F32, 188416)
    MBC = sbat("MBC", [128, SLOC], F32, 188416)
    RBC = sbat("RBC", [128, SLOC], F32, 196608)
    PROD = sbat("PROD", [128, 1024], BF16, 188416)
    PROD2 = sbat("PROD2", [128, 1024], BF16, 190464)
    PB2 = sbat("PB2", [128, 1024], BF16, 194560)
    TMS2 = sbat("TMS2", [128, 1024], BF16, 196608)
    PB3 = sbat("PB3", [128, 1024], BF16, 198656)
    TMS3 = sbat("TMS3", [128, 1024], BF16, 200704)
    E01T = sbat("E01T", [40, 1024], BF16, 192512)
    CORRS = sbat("CORRS", [128, OWN], F32, 188416)
    SUMS = [sbat(f"SUMS{i}", [128, 512], F32, 192512 + 2048 * i) for i in range(2)]
    RECIP = [sbat(f"RECIP{i}", [128, 512], F32, 196608 + 2048 * i) for i in range(2)]
    LNT = sbat("LNT", [128, 512], F32, 200704)
    GTB = [sbat(f"GTB{i}", [128, OWN], F32, 188416 + 4096 * i) for i in range(2)]
    XTT = [sbat(f"XTT{i}", [128, OWN], F32, 196608 + 4096 * i) for i in range(2)]
    W01S = sbat("W01S", [128, 1280], BF16, 204800)
    ENC01 = sbat("ENC01", [128, 1280], BF16, 207360)
    BB = sbat("BB", [128, 32], F32, 209920)
    ONESBF = sbat("ONESBF", [128, 128], BF16, 210048)
    ONESF1 = sbat("ONESF1", [1, 128], F32, 210304)
    EPSC = sbat("EPSC", [1, 1], F32, 210816)
    OUTA = [sbat(f"OUTA{i}", [128, OWN], F32, 65536 + 4096 * i) for i in range(2)]
    OUTB = [sbat(f"OUTB{i}", [128, OWN], F32, 65536 + 8192 + 4096 * i) for i in range(4)]

    def attn_tile(seti, j):
        col = 16384 + (seti * 12 + j) * 512
        return XNO[:, col:col + 512]

    def exptmp_tile(slot):
        col = 16384 + 24 * 512 + slot * 512
        return XNO[:, col:col + 512]

    def finish():
        if taps:
            tsem = stack.enter_context(nc.semaphore("tsem"))
            tmap = {"XNP": (XNP, BF16), "XNO": (XNO, BF16), "KT": (KT, BF16),
                    "VSP": (vsp, BF16), "GSP": (gsp, F32), "A01T": (A01T, BF16),
                    "E01T": (E01T, BF16), "W01S": (W01S, BF16), "COEFC": (COEFC, BF16), "COEFM": (COEFM, BF16)}
            base = dict(T)
            with nc.Block() as b:
                @b.sync
                def _(s):
                    entry(s, base)
                    n = 0
                    for name in taps:
                        t, dt_ = tmap[name]
                        shape = list(t.shape)
                        d = nc.dram_tensor("tap_" + name, shape, dt_,
                                           kind="ExternalOutput").ap()
                        s.dma_start(d, t[:]).then_inc(tsem, 16)
                        n += 16
                    s.wait_ge(tsem, n)
        return nc, stack

    stack = contextlib.ExitStack()
    sem = {}
    for name in ["sIN", "sGP", "sSC", "sCHK", "sPE", "sPEG", "sVE", "sAC",
                 "sSCR", "sEXP", "sMS", "sSUM", "sAV", "sCB", "sRC", "sOE",
                 "sW0", "sW1", "sW2", "sW3", "sP0", "sP1", "sG0", "sG1", "sT2"]:
        sem[name] = stack.enter_context(nc.semaphore(name))
    T = {k: 0 for k in sem}
    ALLSEMS = list(sem.keys())

    def entry(e, base):
        for k in ALLSEMS:
            if base[k] > 0:
                e.wait_ge(sem[k], base[k])

    chsem = {"v": stack.enter_context(nc.semaphore("chv")),
             "s": stack.enter_context(nc.semaphore("chs"))}
    chcnt = {"v": 0, "s": 0}

    def chain(eng, kind, instr):
        chcnt[kind] += 1
        instr.then_inc(chsem[kind], 1)
        eng.wait_ge(chsem[kind], chcnt[kind])

    b0 = dict(T)
    with nc.Block() as b:
        @b.gpsimd
        def _(g):
            g.dma_start(COEFC[:], coefs01.rearrange("(t p) c -> p t c", p=128)).then_inc(sem["sGP"], 16)
            g.dma_start(COEFM[:], mask01.rearrange("(t p) c -> p t c", p=128)).then_inc(sem["sGP"], 16)
            g.dma_start(ENC01[:], enc01T.rearrange("(t p) c -> p t c", p=128)).then_inc(sem["sGP"], 16)
            g.dma_start(BB[:], bbt).then_inc(sem["sGP"], 16)
            g.memset(ONESBF[:], 1.0)
            g.memset(ONESF1[:], 1.0)
            g.memset(EPSC[:], EPS).then_inc(sem["sPEG"], 1)
            g.wait_ge(sem["sGP"], b0["sGP"] + 64)

        @b.vector
        def _(v):
            v.wait_ge(sem["sGP"], b0["sGP"] + 64)
            for t in range(NDT):
                cs = slice(t * 120, (t + 1) * 120)
                chain(v, "v", v.tensor_mul(XSQ[0][:, 0:120], COEFC[:, cs], COEFM[:, cs]))
                chain(v, "v", v.tensor_add(XSQ[0][:, 128:168], XSQ[0][:, 0:40], XSQ[0][:, 40:80]))
                last = v.tensor_add(W01S[:, t * 40:(t + 1) * 40], XSQ[0][:, 128:168], XSQ[0][:, 80:120])
            last.then_inc(sem["sVE"], 1)
    T["sGP"] += 64
    T["sPEG"] += 1
    T["sVE"] += 1

    if upto < 1:
        return finish()
    psS_g = nc.psum_tensor("psS", [1, SLOC], F32)
    psQ_g = nc.psum_tensor("psQ", [1, SLOC], F32)
    gS = psS_g.__enter__()
    gQ = psQ_g.__enter__()
    b1 = dict(T)
    with nc.Block() as b:
        xb3 = [XBUF[0], XBUF[1], XBUF2]
        xs3 = ["sP0", "sP1", "sG0"]

        @b.sync
        def _(s):
            entry(s, b1)
            for dt in range(NDT):
                if dt % 3 == 2:
                    continue
                if dt >= 3:
                    s.wait_ge(sem["sVE"], b1["sVE"] + (dt - 2))
                    s.wait_ge(sem["sAC"], b1["sAC"] + (dt - 2))
                s.dma_start(xb3[dt % 3][:],
                            xT[dt * 128:(dt + 1) * 128, :]).then_inc(sem[xs3[dt % 3]], 16)

        @b.gpsimd
        def _(g):
            entry(g, b1)
            for dt in range(NDT):
                if dt % 3 != 2:
                    continue
                if dt >= 3:
                    g.wait_ge(sem["sVE"], b1["sVE"] + (dt - 2))
                    g.wait_ge(sem["sAC"], b1["sAC"] + (dt - 2))
                g.dma_start(xb3[dt % 3][:],
                            xT[dt * 128:(dt + 1) * 128, :]).then_inc(sem[xs3[dt % 3]], 16)

        @b.vector
        def _(v):
            entry(v, b1)
            for dt in range(NDT):
                v.wait_ge(sem[xs3[dt % 3]], b1[xs3[dt % 3]] + 16 * (dt // 3 + 1))
                v.tensor_copy(XNP[:, dt * 1024:(dt + 1) * 1024], xb3[dt % 3][:, 0:1024])
                v.tensor_copy(XNO[:, dt * 1024:(dt + 1) * 1024], xb3[dt % 3][:, 1024:2048]).then_inc(sem["sVE"], 1)

        @b.scalar
        def _(sc):
            entry(sc, b1)
            for dt in range(NDT):
                sc.wait_ge(sem[xs3[dt % 3]], b1[xs3[dt % 3]] + 16 * (dt // 3 + 1))
                if dt >= 2:
                    sc.wait_ge(sem["sPE"], b1["sPE"] + (dt - 1))
                sc.activation(XSQ[dt % 2][:], xb3[dt % 3][:], AF.Square).then_inc(sem["sAC"], 1)

        @b.tensor
        def _(t):
            entry(t, b1)
            for dt in range(NDT):
                t.wait_ge(sem["sVE"], b1["sVE"] + (dt + 1))
                t.wait_ge(sem["sAC"], b1["sAC"] + (dt + 1))
                for c in range(4):
                    src = XNP if c < 2 else XNO
                    cc = dt * 1024 + (c % 2) * 512
                    t.matmul(gS[0:1, c * 512:(c + 1) * 512], ONESBF[:, 0:1], src[:, cc:cc + 512],
                             start=(dt == 0), stop=(dt == NDT - 1))
                last = None
                for c in range(4):
                    last = t.matmul(gQ[0:1, c * 512:(c + 1) * 512], ONESBF[:, 0:1],
                                    XSQ[dt % 2][:, c * 512:(c + 1) * 512],
                                    start=(dt == 0), stop=(dt == NDT - 1))
                last.then_inc(sem["sPE"], 1)
    T["sP0"] += 16 * 11
    T["sP1"] += 16 * 11
    T["sG0"] += 16 * 10
    T["sVE"] += NDT
    T["sAC"] += NDT
    T["sPE"] += NDT

    bq = dict(T)
    with nc.Block() as b:
        @b.vector
        def _(v):
            entry(v, bq)
            inv = 1.0 / D
            chain(v, "v", v.tensor_scalar(QROWS[0:1, 0:2048], gS[0:1, :], inv, None, OP.mult))
            chain(v, "v", v.tensor_scalar(QROWS[32:33, 0:2048], gQ[0:1, :], inv, None, OP.mult))
            chain(v, "v", v.tensor_mul(QROWS[32:33, 2048:4096], QROWS[0:1, 0:2048], QROWS[0:1, 0:2048]))
            chain(v, "v", v.tensor_sub(QROWS[64:65, 0:2048], QROWS[32:33, 0:2048], QROWS[32:33, 2048:4096]))
            v.tensor_scalar(QROWS[64:65, 2048:4096], QROWS[64:65, 0:2048], EPS, None, OP.add).then_inc(sem["sVE"], 1)

        @b.scalar
        def _(sc):
            entry(sc, bq)
            sc.wait_ge(sem["sVE"], bq["sVE"] + 1)
            chain(sc, "s", sc.activation(QROWS[64:65, 0:2048], QROWS[64:65, 2048:4096], AF.Ln))
            sc.activation(QROWS[0:1, 2048:4096], QROWS[64:65, 0:2048], AF.Exp, scale=-0.5).then_inc(sem["sAC"], 1)
    T["sVE"] += 1
    T["sAC"] += 1
    psQ_g.__exit__(None, None, None)
    psS_g.__exit__(None, None, None)

    psBm_g = nc.psum_tensor("psBm", [128, SLOC], F32)
    psBr_g = nc.psum_tensor("psBr", [128, SLOC], F32)
    gBm = psBm_g.__enter__()
    gBr = psBr_g.__enter__()
    bb_ = dict(T)
    with nc.Block() as b:
        @b.tensor
        def _(t):
            entry(t, bb_)
            last = None
            for c in range(4):
                t.matmul(gBm[:, c * 512:(c + 1) * 512], ONESF1[:], QROWS[0:1, c * 512:(c + 1) * 512],
                         start=True, stop=True)
                last = t.matmul(gBr[:, c * 512:(c + 1) * 512], ONESF1[:],
                                QROWS[0:1, 2048 + c * 512:2048 + (c + 1) * 512], start=True, stop=True)
            last.then_inc(sem["sPE"], 1)

        @b.vector
        def _(v):
            entry(v, bb_)
            v.wait_ge(sem["sPE"], bb_["sPE"] + 1)
            v.tensor_copy(MBCB[:], gBm[:])
            v.tensor_copy(RBCB[:], gBr[:]).then_inc(sem["sVE"], 1)
    T["sPE"] += 1
    T["sVE"] += 1
    psBr_g.__exit__(None, None, None)
    psBm_g.__exit__(None, None, None)

    if upto < 2:
        return finish()
    psA_g = nc.psum_tensor("psA", [40, D], F32)
    gA = psA_g.__enter__()
    b2 = dict(T)
    with nc.Block() as b:
        @b.vector
        def _(v):
            entry(v, b2)
            for dt in range(NDT):
                v.tensor_sub(TMPB[:, 0:1024], XNP[:, dt * 1024:(dt + 1) * 1024], MBCB[:, 0:1024])
                chain(v, "v", v.tensor_sub(TMPB[:, 1024:2048], XNO[:, dt * 1024:(dt + 1) * 1024], MBCB[:, 1024:2048]))
                v.tensor_mul(XNP[:, dt * 1024:(dt + 1) * 1024], TMPB[:, 0:1024], RBCB[:, 0:1024])
                v.tensor_mul(XNO[:, dt * 1024:(dt + 1) * 1024], TMPB[:, 1024:2048], RBCB[:, 1024:2048]).then_inc(sem["sVE"], 3)
            v.wait_ge(sem["sCHK"], b2["sCHK"] + 32)
            v.tensor_copy(A01T[0:40, :], gA[0:40, :]).then_inc(sem["sVE"], 1)

        @b.scalar
        def _(sc):
            entry(sc, b2)
            for p in range(32):
                if p >= 3:
                    sc.wait_ge(sem["sCHK"], b2["sCHK"] + (p - 2))
                sc.dma_start(TTS2[p % 3][:],
                             tth[p * 128:(p + 1) * 128, :]).then_inc(sem["sW%d" % (p % 3)], 16)

        @b.tensor
        def _(t):
            entry(t, b2)
            for p in range(32):
                t.wait_ge(sem["sW%d" % (p % 3)], b2["sW%d" % (p % 3)] + 16 * (p // 3 + 1))
                last = None
                for c in range(8):
                    last = t.matmul(gA[0:40, c * 512:(c + 1) * 512], W01S[:, p * 40:(p + 1) * 40],
                                    TTS2[p % 3][:, c * 512:(c + 1) * 512],
                                    start=(p == 0), stop=(p == 31))
                last.then_inc(sem["sCHK"], 1)
    T["sVE"] += 3 * NDT + 1
    T["sW0"] += 16 * 11
    T["sW1"] += 16 * 11
    T["sW2"] += 16 * 10
    T["sCHK"] += 32
    psA_g.__exit__(None, None, None)

    if upto < 3:
        return finish()
    psE_g = nc.psum_tensor("psE", [40, 1024], F32)
    gE = psE_g.__enter__()
    b3a = dict(T)
    with nc.Block() as b:
        @b.tensor
        def _(t):
            entry(t, b3a)
            last = None
            for dt in range(NDT):
                for c in range(2):
                    last = t.matmul(gE[0:40, c * 512:(c + 1) * 512], ENC01[:, dt * 40:(dt + 1) * 40],
                                    XNO[:, dt * 1024 + c * 512:dt * 1024 + (c + 1) * 512],
                                    start=(dt == 0), stop=(dt == NDT - 1))
            last.then_inc(sem["sPEG"], 1)

        @b.scalar
        def _(sc):
            entry(sc, b3a)
            sc.wait_ge(sem["sPEG"], b3a["sPEG"] + 1)
            sc.copy(E01T[0:40, :], gE[0:40, :]).then_inc(sem["sAC"], 1)
    T["sPEG"] += 1
    T["sAC"] += 1
    psE_g.__exit__(None, None, None)

    psK_g = [nc.psum_tensor(f"psK{i}", [128, 1024], F32) for i in range(4)]
    psK = [g.__enter__() for g in psK_g]
    b3 = dict(T)
    with nc.Block() as b:
        @b.gpsimd
        def _(g):
            entry(g, b3)
            g.dma_start(COSD[:], ropec).then_inc(sem["sT2"], 16)
            g.dma_start(SINSGN[:], ropes).then_inc(sem["sT2"], 16)
            wchr = [WCH[0], WCH[1], PB, TMS]
            for i in range(128):
                g2, rp = i // 16, i % 16
                grp, otp = g2 // 2, g2 % 2
                og = grp % 2
                if i >= 4:
                    g.wait_ge(sem["sCHK"], b3["sCHK"] + (i - 3))
                g.dma_start(wchr[i % 4][:, 0:512].rearrange("p (b c) -> p b c", c=256),
                            wk[rp * 256:(rp + 1) * 256,
                               og * 512 + otp * 256:og * 512 + (otp + 1) * 256].rearrange(
                                "(b p) c -> p b c", p=128)).then_inc(sem["sW%d" % (i % 4)], 16)

        @b.tensor
        def _(t):
            entry(t, b3)
            wchr = [WCH[0], WCH[1], PB, TMS]
            for g2 in range(8):
                grp, otp = g2 // 2, g2 % 2
                sh = grp // 2
                pd = (g2 % 2) * 2
                if g2 >= 2:
                    t.wait_ge(sem["sAC"], b3["sAC"] + 2 * (g2 - 1))
                src = XNP if sh == 0 else XNO
                for rp in range(16):
                    i = g2 * 16 + rp
                    t.wait_ge(sem["sW%d" % (i % 4)], b3["sW%d" % (i % 4)] + 16 * (i // 4 + 1))
                    last = None
                    for bb2 in range(2):
                        dt = 2 * rp + bb2
                        for oti in range(2):
                            for c in range(2):
                                last = t.matmul(psK[pd + oti][:, c * 512:(c + 1) * 512],
                                                wchr[i % 4][:, bb2 * 256 + oti * 128:bb2 * 256 + (oti + 1) * 128],
                                                src[:, dt * 1024 + c * 512:dt * 1024 + (c + 1) * 512],
                                                start=(dt == 0), stop=(dt == NDT - 1))
                    last.then_inc(sem["sCHK"], 1)

        @b.scalar
        def _(sc):
            entry(sc, b3)
            for k in range(16):
                g2, oti = k // 2, k % 2
                ps = psK[(g2 % 2) * 2 + oti]
                sc.wait_ge(sem["sCHK"], b3["sCHK"] + 16 * (g2 + 1))
                if k >= 2:
                    sc.wait_ge(sem["sVE"], b3["sVE"] + 3 * (k - 1))
                pbk = PB2 if k % 2 == 0 else PB3
                tmk = TMS2 if k % 2 == 0 else TMS3
                sc.copy(pbk[:], ps[:])
                sc.copy(tmk[0:64, :], ps[64:128, :])
                sc.copy(tmk[64:128, :], ps[0:64, :]).then_inc(sem["sAC"], 1)

        @b.vector
        def _(v):
            entry(v, b3)
            v.wait_ge(sem["sT2"], b3["sT2"] + 32)
            for k in range(16):
                g2, oti = k // 2, k % 2
                grp, otp = g2 // 2, g2 % 2
                sh, og = grp // 2, grp % 2
                h = og * 4 + otp * 2 + oti
                v.wait_ge(sem["sAC"], b3["sAC"] + (k + 1))
                pbk = PB2 if k % 2 == 0 else PB3
                tmk = TMS2 if k % 2 == 0 else TMS3
                v.tensor_mul(PROD[:], pbk[:], COSD[:, sh * 1024:(sh + 1) * 1024])
                chain(v, "v", v.tensor_mul(PROD2[:], tmk[:], SINSGN[:, sh * 1024:(sh + 1) * 1024]))
                v.tensor_add(KT[:, h * 2048 + sh * 1024:h * 2048 + (sh + 1) * 1024],
                             PROD[:], PROD2[:]).then_inc(sem["sVE"], 3)
    T["sT2"] += 32
    for _i in range(4):
        T["sW%d" % _i] += 16 * 32
    T["sCHK"] += 128
    T["sAC"] += 16
    T["sVE"] += 48

    if upto < 4:
        return finish()
    b4 = dict(T)
    with nc.Block() as b:
        @b.gpsimd
        def _(g):
            entry(g, b4)
            wchr = [WCH[0], WCH[1], PB, TMS]
            for i in range(128):
                if i >= 4:
                    g.wait_ge(sem["sCHK"], b4["sCHK"] + (i - 3))
                dt = i % 32
                g.dma_start(wchr[i % 4][:],
                            wv[dt * 128:(dt + 1) * 128, :]).then_inc(sem["sW%d" % (i % 4)], 16)

        @b.tensor
        def _(t):
            entry(t, b4)
            for grp in range(4):
                if grp >= 1:
                    t.wait_ge(sem["sVE"], b4["sVE"] + 4 * grp)
                wchr = [WCH[0], WCH[1], PB, TMS]
                for dt in range(NDT):
                    i = grp * 32 + dt
                    t.wait_ge(sem["sW%d" % (i % 4)], b4["sW%d" % (i % 4)] + 16 * (i // 4 + 1))
                    last = None
                    for sti in range(4):
                        st = grp * 4 + sti
                        src = XNP if st < 8 else XNO
                        col = dt * 1024 + (st % 8) * 128
                        for c in range(2):
                            last = t.matmul(psK[sti][:, c * 512:(c + 1) * 512],
                                            src[:, col:col + 128],
                                            wchr[i % 4][:, c * 512:(c + 1) * 512],
                                            start=(dt == 0), stop=(dt == NDT - 1))
                    last.then_inc(sem["sCHK"], 1)

        @b.vector
        def _(v):
            entry(v, b4)
            for st in range(16):
                v.wait_ge(sem["sCHK"], b4["sCHK"] + 32 * (st // 4 + 1))
                if st >= 2:
                    v.wait_ge(sem["sSC"], b4["sSC"] + 16 * (st - 1))
                buf = PB2 if st % 2 == 0 else TMS2
                v.tensor_copy(buf[:], psK[st % 4][:]).then_inc(sem["sVE"], 1)

        @b.scalar
        def _(sc):
            entry(sc, b4)
            for st in range(16):
                sc.wait_ge(sem["sVE"], b4["sVE"] + st + 1)
                buf = PB2 if st % 2 == 0 else TMS2
                sc.dma_start(vsp[st * 128:(st + 1) * 128, :], buf[:]).then_inc(sem["sSC"], 16)
            sc.wait_ge(sem["sSC"], b4["sSC"] + 16 * 16)
    for _i in range(4):
        T["sW%d" % _i] += 16 * 32
    T["sCHK"] += 128
    T["sVE"] += 16
    T["sSC"] += 16 * 16

    if upto < 5:
        return finish()
    b5 = dict(T)
    with nc.Block() as b:
        @b.gpsimd
        def _(g):
            entry(g, b5)
            wchr = [WCH[0], WCH[1], PB, TMS]
            for i in range(256):
                if i >= 4:
                    g.wait_ge(sem["sCHK"], b5["sCHK"] + (i - 3))
                og2, rp = i // 16, i % 16
                og, otp = og2 // 2, og2 % 2
                g.dma_start(wchr[i % 4][:, 0:512].rearrange("p (b c) -> p b c", c=256),
                            wq[rp * 256:(rp + 1) * 256,
                               og * 512 + otp * 256:og * 512 + (otp + 1) * 256].rearrange(
                                "(b p) c -> p b c", p=128)).then_inc(sem["sW%d" % (i % 4)], 16)

        @b.tensor
        def _(t):
            entry(t, b5)
            wchr = [WCH[0], WCH[1], PB, TMS]
            for og2 in range(16):
                pd = (og2 % 2) * 2
                if og2 >= 2:
                    t.wait_ge(sem["sAC"], b5["sAC"] + 2 * (og2 - 1))
                for rp in range(16):
                    i = og2 * 16 + rp
                    t.wait_ge(sem["sW%d" % (i % 4)], b5["sW%d" % (i % 4)] + 16 * (i // 4 + 1))
                    last = None
                    for bb2 in range(2):
                      dt = 2 * rp + bb2
                      for oti in range(2):
                        for c in range(2):
                            last = t.matmul(psK[pd + oti][:, c * 512:(c + 1) * 512],
                                            wchr[i % 4][:, bb2 * 256 + oti * 128:bb2 * 256 + (oti + 1) * 128],
                                            XNO[:, dt * 1024 + c * 512:dt * 1024 + (c + 1) * 512],
                                            start=(dt == 0), stop=(dt == NDT - 1))
                    last.then_inc(sem["sCHK"], 1)

        @b.scalar
        def _(sc):
            entry(sc, b5)
            for k in range(32):
                og2, oti = k // 2, k % 2
                ps = psK[(og2 % 2) * 2 + oti]
                sc.wait_ge(sem["sCHK"], b5["sCHK"] + 16 * (og2 + 1))
                if k >= 2:
                    sc.wait_ge(sem["sVE"], b5["sVE"] + 3 * (k - 1))
                pbk = PB2 if k % 2 == 0 else PB3
                tmk = TMS2 if k % 2 == 0 else TMS3
                sc.copy(pbk[:], ps[:])
                sc.copy(tmk[0:64, :], ps[64:128, :])
                sc.copy(tmk[64:128, :], ps[0:64, :]).then_inc(sem["sAC"], 1)

        @b.vector
        def _(v):
            entry(v, b5)
            for k in range(32):
                hg = k
                v.wait_ge(sem["sAC"], b5["sAC"] + (k + 1))
                pbk = PB2 if k % 2 == 0 else PB3
                tmk = TMS2 if k % 2 == 0 else TMS3
                v.tensor_mul(PROD[:], pbk[:], COSD[:, 1024:2048])
                chain(v, "v", v.tensor_mul(PROD2[:], tmk[:], SINSGN[:, 1024:2048]))
                v.tensor_add(XNP[:, hg * 1024:(hg + 1) * 1024], PROD[:], PROD2[:]).then_inc(sem["sVE"], 3)
    for _i in range(4):
        T["sW%d" % _i] += 16 * 64
    T["sCHK"] += 256
    T["sAC"] += 32
    T["sVE"] += 96

    if upto < 6:
        return finish()
    b6 = dict(T)
    with nc.Block() as b:
        @b.gpsimd
        def _(g):
            entry(g, b6)
            wchr = [WCH[0], WCH[1], PB, TMS]
            for i in range(256):
                if i >= 4:
                    g.wait_ge(sem["sCHK"], b6["sCHK"] + (i - 3))
                og2, rp = i // 16, i % 16
                og, otp = og2 // 2, og2 % 2
                g.dma_start(wchr[i % 4][:, 0:512].rearrange("p (b c) -> p b c", c=256),
                            wbT[rp * 256:(rp + 1) * 256,
                                og * 512 + otp * 256:og * 512 + (otp + 1) * 256].rearrange(
                                "(b p) c -> p b c", p=128)).then_inc(sem["sW%d" % (i % 4)], 16)

        @b.tensor
        def _(t):
            entry(t, b6)
            wchr = [WCH[0], WCH[1], PB, TMS]
            for og2 in range(16):
                pd = (og2 % 2) * 2
                if og2 >= 2:
                    t.wait_ge(sem["sAC"], b6["sAC"] + 2 * (og2 - 1))
                for rp in range(16):
                    i = og2 * 16 + rp
                    t.wait_ge(sem["sW%d" % (i % 4)], b6["sW%d" % (i % 4)] + 16 * (i // 4 + 1))
                    last = None
                    for bb2 in range(2):
                      dt = 2 * rp + bb2
                      for oti in range(2):
                        for c in range(2):
                            last = t.matmul(psK[pd + oti][:, c * 512:(c + 1) * 512],
                                            wchr[i % 4][:, bb2 * 256 + oti * 128:bb2 * 256 + (oti + 1) * 128],
                                            XNO[:, dt * 1024 + c * 512:dt * 1024 + (c + 1) * 512],
                                            start=(dt == 0), stop=False)
                    last.then_inc(sem["sCHK"], 1)
                for oti in range(2):
                    k = og2 * 2 + oti
                    last = None
                    for c in range(2):
                        last = t.matmul(psK[pd + oti][:, c * 512:(c + 1) * 512],
                                        A01T[0:40, k * 128:(k + 1) * 128],
                                        E01T[0:40, c * 512:(c + 1) * 512],
                                        start=False, stop=True)
                    last.then_inc(sem["sPE"], 1)

        @b.scalar
        def _(sc):
            entry(sc, b6)
            for k in range(32):
                og2, oti = k // 2, k % 2
                ps = psK[(og2 % 2) * 2 + oti]
                sc.wait_ge(sem["sPE"], b6["sPE"] + k + 1)
                if k >= 2:
                    sc.wait_ge(sem["sG%d" % (k % 2)], b6["sG%d" % (k % 2)] + 16 * (k // 2))
                sc.activation(G[k % 2][:], ps[:], AF.Silu, bias=BB[:, k:k + 1]).then_inc(sem["sAC"], 1)
                sc.wait_ge(sem["sAC"], b6["sAC"] + k + 1)
                sc.dma_start(gsp[k * 128:(k + 1) * 128, :], G[k % 2][:]).then_inc(sem["sG%d" % (k % 2)], 16)
            sc.wait_ge(sem["sG0"], b6["sG0"] + 16 * 16)
            sc.wait_ge(sem["sG1"], b6["sG1"] + 16 * 16)
    for _i in range(4):
        T["sW%d" % _i] += 16 * 64
    T["sCHK"] += 256
    T["sPE"] += 32
    T["sAC"] += 32
    T["sG0"] += 16 * 16
    T["sG1"] += 16 * 16
    for g in reversed(psK_g):
        g.__exit__(None, None, None)

    if upto < 7:
        return finish()
    scp_g = [nc.psum_tensor(f"scp{i}", [128, 1024], F32) for i in range(2)]
    scp = [g.__enter__() for g in scp_g]
    sums_g = [nc.psum_tensor("sums0", [128, 512], F32)]
    sums_ps = [g.__enter__() for g in sums_g]
    av_g = [nc.psum_tensor(f"av{i}", [128, 512], F32) for i in range(3)]
    av_ps = [g.__enter__() for g in av_g]
    units = [(h, g, qc) for h in range(8) for g in range(4) for qc in range(2)]
    units = units[:nunits]
    NU = len(units)
    b7 = dict(T)

    def attn_pair(seti, jp):
        col = 16384 + (seti * 12 + 2 * jp) * 512
        return XNO[:, col:col + 1024]

    def exptmp_pair(i):
        col = 16384 + 24 * 512 + i * 1024
        return XNO[:, col:col + 1024]

    ORD = [0, 1, 4, 5, 2, 3]

    def qlo(j):
        return 128 * (j - 8) if j >= 8 else 0

    def qhi(j):
        return min(512, 128 * (j + 1)) if j < 4 else 512

    def unit_scores(t, u):
        h, g_, qc = units[u]
        hg = h * 4 + g_
        for pos, jp in enumerate(ORD):
            tgt = 6 * u + pos - 1
            if tgt > 0:
                t.wait_ge(sem["sEXP"], b7["sEXP"] + tgt)
            for jj in range(2):
                j = 2 * jp + jj
                st = 4 * qc + j
                lo, hi = qlo(j), qhi(j)
                t.matmul(scp[pos % 2][:, jj * 512 + lo:jj * 512 + hi],
                         KT[:, h * 2048 + st * 128:h * 2048 + (st + 1) * 128],
                         XNP[:, hg * 1024 + qc * 512 + lo:hg * 1024 + qc * 512 + hi],
                         start=True, stop=True).then_inc(sem["sSCR"], 1)

    with nc.Block() as b:
        @b.sync
        def _(s):
            entry(s, b7)
            for st in range(16):
                s.dma_start(XNO[:, st * 1024:(st + 1) * 1024],
                            vsp[st * 128:(st + 1) * 128, :]).then_inc(sem["sIN"], 16)

        @b.gpsimd
        def _(g):
            entry(g, b7)
            for j in range(8):
                g.dma_start(MASKS[:, j * 512:(j + 1) * 512],
                            masks8[j * 128:(j + 1) * 128, :]).then_inc(sem["sGP"], 16)
            g.dma_start(CORRS[:], corr).then_inc(sem["sGP"], 16)

        @b.tensor
        def _(t):
            entry(t, b7)
            if NU > 0:
                unit_scores(t, 0)
            for u in range(NU):
                h, g_, qc = units[u]
                if u + 1 < NU:
                    unit_scores(t, u + 1)
                t.wait_ge(sem["sEXP"], b7["sEXP"] + 6 * u + 6)
                t.wait_ge(sem["sMS"], b7["sMS"] + 4 * u + 4)
                if u >= 1:
                    t.wait_ge(sem["sCB"], b7["sCB"] + u)
                last = None
                for j in range(12):
                    lo, hi = qlo(j), qhi(j)
                    last = t.matmul(sums_ps[0][:, lo:hi], ONESBF[:],
                                    attn_tile(u % 2, j)[:, lo:hi],
                                    start=(j == 0), stop=(j == 11))
                last.then_inc(sem["sSUM"], 1)
                if u == 0:
                    t.wait_ge(sem["sIN"], b7["sIN"] + 16 * 16)
                if u >= 3:
                    t.wait_ge(sem["sOE"], b7["sOE"] + u - 2)
                last = None
                for j in range(12):
                    st = 4 * qc + j
                    lo, hi = qlo(j), qhi(j)
                    last = t.matmul(av_ps[u % 3][:, lo:hi],
                                    XNO[:, st * 1024 + h * 128:st * 1024 + (h + 1) * 128],
                                    attn_tile(u % 2, j)[:, lo:hi], start=(j == 0), stop=(j == 11))
                last.then_inc(sem["sAV"], 1)

        @b.scalar
        def _(sc):
            entry(sc, b7)
            def recip(u):
                sc.wait_ge(sem["sCB"], b7["sCB"] + u + 1)
                if u >= 2:
                    sc.wait_ge(sem["sOE"], b7["sOE"] + u - 1)
                chain(sc, "s", sc.activation(LNT[:], SUMS[u % 2][:], AF.Ln))
                sc.activation(RECIP[u % 2][:], LNT[:], AF.Exp, scale=-1.0).then_inc(sem["sRC"], 1)
            for u in range(NU):
                for pos, jp in enumerate(ORD):
                    sc.wait_ge(sem["sSCR"], b7["sSCR"] + 12 * u + 2 * pos + 2)
                    if pos == 0 and u >= 2:
                        sc.wait_ge(sem["sAV"], b7["sAV"] + u - 1)
                    if jp < 2:
                        tgt = 4 * u + jp - 1
                        if tgt > 0:
                            sc.wait_ge(sem["sMS"], b7["sMS"] + tgt)
                        dst = exptmp_pair(jp)
                    elif jp >= 4:
                        sc.wait_ge(sem["sMS"], b7["sMS"] + 4 * u + jp - 3)
                        dst = exptmp_pair(jp - 4)
                    else:
                        dst = attn_pair(u % 2, jp)
                    sc.activation(dst, scp[pos % 2][:], AF.Exp, scale=SM).then_inc(sem["sEXP"], 1)
                if u >= 1:
                    recip(u - 1)
            if NU > 0:
                recip(NU - 1)

        @b.vector
        def _(v):
            entry(v, b7)
            v.wait_ge(sem["sGP"], b7["sGP"] + 16 * 9)

            def vmasks(u):
                for mp in range(4):
                    jp = mp if mp < 2 else mp + 2
                    v.wait_ge(sem["sEXP"], b7["sEXP"] + 6 * u + mp + 1)
                    if mp == 0 and u >= 2:
                        v.wait_ge(sem["sAV"], b7["sAV"] + u - 1)
                    v.tensor_mul(attn_pair(u % 2, jp), exptmp_pair(jp % 2 if jp < 2 else jp - 4),
                                 MASKS[:, mp * 1024:(mp + 1) * 1024]).then_inc(sem["sMS"], 1)

            if NU > 0:
                vmasks(0)
            for u in range(NU):
                h, g_, qc = units[u]
                if u + 1 < NU:
                    vmasks(u + 1)
                v.wait_ge(sem["sSUM"], b7["sSUM"] + u + 1)
                if u >= 2:
                    v.wait_ge(sem["sRC"], b7["sRC"] + u - 1)
                v.tensor_sub(SUMS[u % 2][:], sums_ps[0][:],
                             CORRS[:, qc * 512:(qc + 1) * 512]).then_inc(sem["sCB"], 1)
                v.wait_ge(sem["sAV"], b7["sAV"] + u + 1)
                v.wait_ge(sem["sRC"], b7["sRC"] + u + 1)
                hg = h * 4 + g_
                v.tensor_mul(XNP[:, hg * 1024 + qc * 512:hg * 1024 + (qc + 1) * 512],
                             av_ps[u % 3][:], RECIP[u % 2][:]).then_inc(sem["sOE"], 1)
    T["sIN"] += 16 * 16
    T["sGP"] += 16 * 9
    T["sSCR"] += 12 * NU
    T["sEXP"] += 6 * NU
    T["sMS"] += 4 * NU
    T["sSUM"] += NU
    T["sAV"] += NU
    T["sCB"] += NU
    T["sRC"] += NU
    T["sOE"] += NU
    for g in reversed(scp_g + sums_g + av_g):
        g.__exit__(None, None, None)

    if upto < 8:
        return finish()
    psO_g = [nc.psum_tensor(f"psO{i}", [128, 1024], F32) for i in range(4)]
    psO = [g.__enter__() for g in psO_g]
    b8 = dict(T)
    with nc.Block() as b:
        @b.gpsimd
        def _(g):
            entry(g, b8)
            wchr = [WCH[0], WCH[1], PB, TMS]
            for i in range(256):
                if i >= 4:
                    g.wait_ge(sem["sCHK"], b8["sCHK"] + (i - 3))
                og2, rp = i // 16, i % 16
                og, otp = og2 // 2, og2 % 2
                g.dma_start(wchr[i % 4][:, 0:512].rearrange("p (b c) -> p b c", c=256),
                            wo[rp * 256:(rp + 1) * 256,
                               og * 512 + otp * 256:og * 512 + (otp + 1) * 256].rearrange(
                                "(b p) c -> p b c", p=128)).then_inc(sem["sW%d" % (i % 4)], 16)

        @b.scalar
        def _(sc):
            entry(sc, b8)
            for k in range(32):
                if k >= 2:
                    sc.wait_ge(sem["sVE"], b8["sVE"] + 2 * (k - 2) + 2)
                sc.dma_start(GTB[k % 2][:], gsp[k * 128:(k + 1) * 128, :]).then_inc(sem["sP%d" % (k % 2)], 16)
                sc.dma_start(XTT[k % 2][:], xT[k * 128:(k + 1) * 128, 1024:2048]).then_inc(sem["sP%d" % (k % 2)], 16)

        @b.tensor
        def _(t):
            entry(t, b8)
            wchr = [WCH[0], WCH[1], PB, TMS]
            for og2 in range(16):
                pd = (og2 % 2) * 2
                if og2 >= 2:
                    t.wait_ge(sem["sVE"], b8["sVE"] + 4 * (og2 - 1))
                for rp in range(16):
                    i = og2 * 16 + rp
                    t.wait_ge(sem["sW%d" % (i % 4)], b8["sW%d" % (i % 4)] + 16 * (i // 4 + 1))
                    last = None
                    for bb2 in range(2):
                      ht = 2 * rp + bb2
                      for oti in range(2):
                        for c in range(2):
                            last = t.matmul(psO[pd + oti][:, c * 512:(c + 1) * 512],
                                            wchr[i % 4][:, bb2 * 256 + oti * 128:bb2 * 256 + (oti + 1) * 128],
                                            XNP[:, ht * 1024 + c * 512:ht * 1024 + (c + 1) * 512],
                                            start=(ht == 0), stop=(ht == NDT - 1))
                    last.then_inc(sem["sCHK"], 1)

        @b.vector
        def _(v):
            entry(v, b8)
            for k in range(32):
                og2, oti = k // 2, k % 2
                v.wait_ge(sem["sCHK"], b8["sCHK"] + 16 * (og2 + 1))
                v.wait_ge(sem["sP%d" % (k % 2)], b8["sP%d" % (k % 2)] + 32 * (k // 2 + 1))
                if k >= 4:
                    v.wait_ge(sem["sG%d" % (k % 2)], b8["sG%d" % (k % 2)] + 16 * (k // 2 - 1))
                pso = psO[(og2 % 2) * 2 + oti]
                chain(v, "v", v.tensor_mul(OUTA[k % 2][:], pso[:], GTB[k % 2][:]))
                v.tensor_add(OUTB[k % 4][:], OUTA[k % 2][:], XTT[k % 2][:]).then_inc(sem["sVE"], 2)

        @b.sync
        def _(s):
            entry(s, b8)
            for k in range(32):
                s.wait_ge(sem["sVE"], b8["sVE"] + 2 * (k + 1))
                s.dma_start(out_t[k * 128:(k + 1) * 128, :], OUTB[k % 4][:]).then_inc(sem["sG%d" % (k % 2)], 16)
            s.wait_ge(sem["sG0"], b8["sG0"] + 16 * 16)
            s.wait_ge(sem["sG1"], b8["sG1"] + 16 * 16)
    for _i in range(4):
        T["sW%d" % _i] += 16 * 64
    T["sCHK"] += 256
    T["sP0"] += 16 * 32
    T["sP1"] += 16 * 32
    T["sVE"] += 64
    T["sG0"] += 16 * 16
    T["sG1"] += 16 * 16
    for g in reversed(psO_g):
        g.__exit__(None, None, None)

    return finish()


_CACHE = {}


def prep_in_maps(L0_attn_x, L0_attn_wq, L0_attn_wk, L0_attn_wv, L0_attn_wo,
                 L0_ffn_W_base, L0_ffn_b_base, L0_ffn_enc0, L0_ffn_enc1,
                 L0_ffn_coefs0, L0_ffn_coefs1, L0_ffn_mask0, L0_ffn_mask1):
    x2d = np.asarray(L0_attn_x, dtype=np.float32).reshape(S, D)

    wq = np.ascontiguousarray(np.asarray(L0_attn_wq, np.float32))
    wk = np.ascontiguousarray(np.asarray(L0_attn_wk, np.float32))
    wv = np.ascontiguousarray(np.asarray(L0_attn_wv, np.float32))
    wo = np.ascontiguousarray(np.asarray(L0_attn_wo, np.float32))
    wbT = np.ascontiguousarray(np.asarray(L0_ffn_W_base, np.float32).T)
    bbt = np.ascontiguousarray(np.asarray(L0_ffn_b_base, np.float32).reshape(32, 128).T)
    enc01T = np.ascontiguousarray(np.concatenate(
        [np.asarray(L0_ffn_enc0, np.float32), np.asarray(L0_ffn_enc1, np.float32)], axis=0).T)
    c0 = np.asarray(L0_ffn_coefs0, np.float32)
    c1 = np.asarray(L0_ffn_coefs1, np.float32)
    m0 = np.asarray(L0_ffn_mask0).astype(np.uint8)
    m1 = np.asarray(L0_ffn_mask1).astype(np.uint8)
    coefs01 = np.ascontiguousarray(np.concatenate(
        [c0[0], c1[0], c0[1], c1[1], c0[2], c1[2]], axis=1))
    mask01 = np.ascontiguousarray(np.concatenate(
        [m0[0], m1[0], m0[1], m1[1], m0[2], m1[2]], axis=1))

    tth = _tth()
    masks8 = _masks8()

    in_maps = []
    for core in range(NCORE):
        lo = core * OWN
        xTc = np.zeros((SLOC, D), np.float32)
        s0 = max(0, lo - 1024)
        xTc[SLOC - (lo + 1024 - s0):] = x2d[s0:lo + 1024]
        xTc = np.ascontiguousarray(xTc.T)
        cosd, sinsgn = _rope_tables(core)
        in_maps.append({
            "xT": xTc, "wq": wq, "wk": wk, "wv": wv, "wo": wo, "wbT": wbT,
            "bbt": bbt, "enc01T": enc01T, "coefs01": coefs01, "mask01": mask01,
            "tth": tth, "ropec": cosd, "ropes": sinsgn, "masks8": masks8,
            "corr": _corr(core),
        })
    return in_maps


def kernel(**inputs):
    in_maps = prep_in_maps(**inputs)
    if "nc" not in _CACHE:
        _CACHE["nc"] = build_nc()
    nc, _stack = _CACHE["nc"]
    res = run_bass_kernel_spmd(nc, in_maps, core_ids=list(range(NCORE)))
    out = np.concatenate([np.ascontiguousarray(r["outT"].T) for r in res.results], axis=0)
    return out.reshape(1, S, D)


# version 16
# speedup vs baseline: 1.1247x; 1.1247x over previous
import math
import contextlib
import numpy as np
import ml_dtypes

import concourse.bass as bass
import concourse.mybir as mybir
from concourse.bass_utils import run_bass_kernel_spmd

F32 = mybir.dt.float32
BF16 = mybir.dt.bfloat16
U8 = mybir.dt.uint8
AF = mybir.ActivationFunctionType
OP = mybir.AluOpType
BF = ml_dtypes.bfloat16

D = 4096
S = 8192
NCORE = 8
OWN = 1024
SLOC = 2048
NDT = 32
SM = 1.0 / math.sqrt(128.0)
EPS = 1e-5



def _tth():
    n = D
    sqrt_n = math.sqrt(n)
    pair_scale = math.sqrt(n / 2.0)
    max_f = (n - 1) // 2
    eye = np.eye(n, dtype=np.float64)
    dc = (eye[:, 0] * sqrt_n)[:, None].astype(np.complex128)
    mid = eye[:, 1:1 + 2 * max_f:2] * pair_scale - 1j * (eye[:, 2:2 + 2 * max_f:2] * pair_scale)
    ny = (eye[:, -1] * sqrt_n)[:, None].astype(np.complex128)
    spectrum = np.concatenate([dc, mid, ny], axis=1)
    T = np.fft.irfft(spectrum, n=n, axis=1).T
    return np.ascontiguousarray((0.5 * T).T).astype(BF)


def _masks8():
    s = np.arange(128)[:, None]
    q = np.arange(512)[None, :]
    out = np.zeros((8, 128, 512), np.float32)
    for j in range(4):
        out[j] = (128 * j + s > q)
    for j in range(8, 12):
        out[j - 4] = (128 * j + s <= q + 1024)
    return np.ascontiguousarray(out.reshape(8 * 128, 512)).astype(BF)


def _rope_tables(core):
    half = 64
    inv = 10000.0 ** (-np.arange(half, dtype=np.float64) / half)
    pos = np.arange(core * OWN - 1024, core * OWN + 1024, dtype=np.float64)
    ang = inv[:, None] * pos[None, :]
    cos, sin = np.cos(ang), np.sin(ang)
    cosd = np.concatenate([cos, cos], axis=0).astype(BF)
    sinsgn = np.concatenate([-sin, sin], axis=0).astype(BF)
    return np.ascontiguousarray(cosd), np.ascontiguousarray(sinsgn)


def _corr(core):
    c = np.zeros(OWN, np.float32)
    if core == 0:
        c = np.maximum(0, 1023 - np.arange(OWN)).astype(np.float32)
    return np.ascontiguousarray(np.broadcast_to(c[None, :], (128, OWN)))



def build_nc(upto=99, taps=(), nunits=64):
    nc = bass.Bass("TRN2", target_bir_lowering=False)

    xT = nc.dram_tensor("xT", [D, SLOC], F32, kind="ExternalInput").ap()
    wq = nc.dram_tensor("wq", [D, D], F32, kind="ExternalInput").ap()
    wk = nc.dram_tensor("wk", [D, 1024], F32, kind="ExternalInput").ap()
    wv = nc.dram_tensor("wv", [D, 1024], F32, kind="ExternalInput").ap()
    wo = nc.dram_tensor("wo", [D, D], F32, kind="ExternalInput").ap()
    wbT = nc.dram_tensor("wbT", [D, D], F32, kind="ExternalInput").ap()
    bbt = nc.dram_tensor("bbt", [128, 32], F32, kind="ExternalInput").ap()
    enc01T = nc.dram_tensor("enc01T", [D, 40], F32, kind="ExternalInput").ap()
    coefs01 = nc.dram_tensor("coefs01", [D, 120], F32, kind="ExternalInput").ap()
    mask01 = nc.dram_tensor("mask01", [D, 120], U8, kind="ExternalInput").ap()
    tth = nc.dram_tensor("tth", [D, D], BF16, kind="ExternalInput").ap()
    ropec = nc.dram_tensor("ropec", [128, SLOC], BF16, kind="ExternalInput").ap()
    ropes = nc.dram_tensor("ropes", [128, SLOC], BF16, kind="ExternalInput").ap()
    masks8 = nc.dram_tensor("masks8", [8 * 128, 512], BF16, kind="ExternalInput").ap()
    corr = nc.dram_tensor("corr", [128, OWN], F32, kind="ExternalInput").ap()
    vsp = nc.dram_tensor("vsp", [SLOC, 1024], BF16).ap()
    gsp = nc.dram_tensor("gsp", [D, OWN], F32).ap()
    out_t = nc.dram_tensor("outT", [D, OWN], F32, kind="ExternalOutput").ap()

    arena = nc.alloc_sbuf_tensor("arena", [128, 210848], U8)
    BASE = nc.lookup_mloc(arena).addr

    def sbat(name, shape, dtype, off):
        return nc.alloc_sbuf_tensor_at(name, shape, dtype, offset=BASE + off)

    XNP = sbat("XNP", [128, 32768], BF16, 0)
    XNO = sbat("XNO", [128, 32768], BF16, 65536)
    KT = sbat("KT", [128, 16384], BF16, 131072)
    COEFC = sbat("COEFC", [128, 3840], BF16, 131072)
    COEFM = sbat("COEFM", [128, 3840], BF16, 138752)
    XBUF = [sbat(f"XBUF{i}", [128, SLOC], F32, 131072 + 8192 * i) for i in range(2)]
    TTB = [sbat(f"TTB{i}", [128, 2048], BF16, 147456 + 4096 * i) for i in range(2)]
    TTS2 = [sbat("TTS2a", [128, 4096], BF16, 147456), sbat("TTS2b", [128, 4096], BF16, 155648),
            sbat("TTS2c", [128, 4096], BF16, 163840)]
    TMPB = sbat("TMPB", [128, SLOC], BF16, 131072)
    MBCB = sbat("MBCB", [128, SLOC], BF16, 139264)
    RBCB = sbat("RBCB", [128, SLOC], BF16, 143360)
    XBUF2 = sbat("XBUF2", [128, SLOC], F32, 147456)
    XSQ = [sbat(f"XSQ{i}", [128, SLOC], BF16, 155648 + 4096 * i) for i in range(2)]
    COSD = sbat("COSD", [128, SLOC], BF16, 163840)
    SINSGN = sbat("SINSGN", [128, SLOC], BF16, 167936)
    TMPF = sbat("TMPF", [128, SLOC], F32, 163840)
    G = [sbat(f"G{i}", [128, OWN], F32, 163840 + 4096 * i) for i in range(2)]
    A01T = sbat("A01T", [40, D], BF16, 172032)
    MASKS = sbat("MASKS", [128, 4096], BF16, 172032)
    WCH = [sbat(f"WCH{i}", [128, 1024], BF16, 180224 + 2048 * i) for i in range(2)]
    PB = sbat("PB", [128, 1024], BF16, 184320)
    TMS = sbat("TMS", [128, 1024], BF16, 186368)
    QROWS = sbat("QROWS", [128, 4096], F32, 188416)
    MBC = sbat("MBC", [128, SLOC], F32, 188416)
    RBC = sbat("RBC", [128, SLOC], F32, 196608)
    PROD = sbat("PROD", [128, 1024], BF16, 188416)
    PROD2 = sbat("PROD2", [128, 1024], BF16, 190464)
    PB2 = sbat("PB2", [128, 1024], BF16, 194560)
    TMS2 = sbat("TMS2", [128, 1024], BF16, 196608)
    PB3 = sbat("PB3", [128, 1024], BF16, 198656)
    TMS3 = sbat("TMS3", [128, 1024], BF16, 200704)
    AT2A = sbat("AT2A", [128, 4096], BF16, 163840)
    E01T = sbat("E01T", [40, 1024], BF16, 192512)
    CORRS = sbat("CORRS", [128, OWN], F32, 188416)
    SUMS = [sbat(f"SUMS{i}", [128, 512], F32, 192512 + 2048 * i) for i in range(2)]
    RECIP = [sbat(f"RECIP{i}", [128, 512], F32, 196608 + 2048 * i) for i in range(2)]
    LNT = sbat("LNT", [128, 512], F32, 200704)
    GTB = [sbat(f"GTB{i}", [128, OWN], F32, 188416 + 4096 * i) for i in range(2)]
    XTT = [sbat(f"XTT{i}", [128, OWN], F32, 196608 + 4096 * i) for i in range(2)]
    W01S = sbat("W01S", [128, 1280], BF16, 204800)
    ENC01 = sbat("ENC01", [128, 1280], BF16, 207360)
    BB = sbat("BB", [128, 32], F32, 209920)
    ONESBF = sbat("ONESBF", [128, 128], BF16, 210048)
    ONESF1 = sbat("ONESF1", [1, 128], F32, 210304)
    EPSC = sbat("EPSC", [1, 1], F32, 210816)
    OUTA = [sbat(f"OUTA{i}", [128, OWN], F32, 65536 + 4096 * i) for i in range(2)]
    OUTB = [sbat(f"OUTB{i}", [128, OWN], F32, 65536 + 8192 + 4096 * i) for i in range(4)]

    def attn_tile(seti, j):
        if seti == 2:
            if j < 8:
                return AT2A[:, j * 512:(j + 1) * 512]
            return XNO[:, 30720 + (j - 8) * 512:30720 + (j - 7) * 512]
        col = 16384 + (seti * 12 + j) * 512
        return XNO[:, col:col + 512]

    def exptmp_tile(slot):
        col = 16384 + 24 * 512 + slot * 512
        return XNO[:, col:col + 512]

    def finish():
        if taps:
            tsem = stack.enter_context(nc.semaphore("tsem"))
            tmap = {"XNP": (XNP, BF16), "XNO": (XNO, BF16), "KT": (KT, BF16),
                    "VSP": (vsp, BF16), "GSP": (gsp, F32), "A01T": (A01T, BF16),
                    "E01T": (E01T, BF16), "W01S": (W01S, BF16), "COEFC": (COEFC, BF16), "COEFM": (COEFM, BF16)}
            base = dict(T)
            with nc.Block() as b:
                @b.sync
                def _(s):
                    entry(s, base)
                    n = 0
                    for name in taps:
                        t, dt_ = tmap[name]
                        shape = list(t.shape)
                        d = nc.dram_tensor("tap_" + name, shape, dt_,
                                           kind="ExternalOutput").ap()
                        s.dma_start(d, t[:]).then_inc(tsem, 16)
                        n += 16
                    s.wait_ge(tsem, n)
        return nc, stack

    stack = contextlib.ExitStack()
    sem = {}
    for name in ["sIN", "sGP", "sSC", "sCHK", "sPE", "sPEG", "sVE", "sAC",
                 "sSCR", "sEXP", "sMS", "sSUM", "sAV", "sCB", "sRC", "sOE",
                 "sW0", "sW1", "sW2", "sW3", "sP0", "sP1", "sG0", "sG1", "sT2"]:
        sem[name] = stack.enter_context(nc.semaphore(name))
    T = {k: 0 for k in sem}
    ALLSEMS = list(sem.keys())

    def entry(e, base):
        for k in ALLSEMS:
            if base[k] > 0:
                e.wait_ge(sem[k], base[k])

    chsem = {"v": stack.enter_context(nc.semaphore("chv")),
             "s": stack.enter_context(nc.semaphore("chs"))}
    chcnt = {"v": 0, "s": 0}

    def chain(eng, kind, instr):
        chcnt[kind] += 1
        instr.then_inc(chsem[kind], 1)
        eng.wait_ge(chsem[kind], chcnt[kind])

    b0 = dict(T)
    with nc.Block() as b:
        @b.gpsimd
        def _(g):
            g.dma_start(COEFC[:], coefs01.rearrange("(t p) c -> p t c", p=128)).then_inc(sem["sGP"], 16)
            g.dma_start(COEFM[:], mask01.rearrange("(t p) c -> p t c", p=128)).then_inc(sem["sGP"], 16)
            g.dma_start(ENC01[:], enc01T.rearrange("(t p) c -> p t c", p=128)).then_inc(sem["sGP"], 16)
            g.dma_start(BB[:], bbt).then_inc(sem["sGP"], 16)
            g.memset(ONESBF[:], 1.0)
            g.memset(ONESF1[:], 1.0)
            g.memset(EPSC[:], EPS).then_inc(sem["sPEG"], 1)
            g.wait_ge(sem["sGP"], b0["sGP"] + 64)

        @b.vector
        def _(v):
            v.wait_ge(sem["sGP"], b0["sGP"] + 64)
            for t in range(NDT):
                cs = slice(t * 120, (t + 1) * 120)
                chain(v, "v", v.tensor_mul(XSQ[0][:, 0:120], COEFC[:, cs], COEFM[:, cs]))
                chain(v, "v", v.tensor_add(XSQ[0][:, 128:168], XSQ[0][:, 0:40], XSQ[0][:, 40:80]))
                last = v.tensor_add(W01S[:, t * 40:(t + 1) * 40], XSQ[0][:, 128:168], XSQ[0][:, 80:120])
            last.then_inc(sem["sVE"], 1)
    T["sGP"] += 64
    T["sPEG"] += 1
    T["sVE"] += 1

    if upto < 1:
        return finish()
    psS_g = nc.psum_tensor("psS", [1, SLOC], F32)
    psQ_g = nc.psum_tensor("psQ", [1, SLOC], F32)
    gS = psS_g.__enter__()
    gQ = psQ_g.__enter__()
    b1 = dict(T)
    with nc.Block() as b:
        xb3 = [XBUF[0], XBUF[1], XBUF2]
        xs3 = ["sP0", "sP1", "sG0"]

        @b.sync
        def _(s):
            entry(s, b1)
            for dt in range(NDT):
                if dt % 3 == 2:
                    continue
                if dt >= 3:
                    s.wait_ge(sem["sVE"], b1["sVE"] + (dt - 2))
                    s.wait_ge(sem["sAC"], b1["sAC"] + (dt - 2))
                s.dma_start(xb3[dt % 3][:],
                            xT[dt * 128:(dt + 1) * 128, :]).then_inc(sem[xs3[dt % 3]], 16)

        @b.gpsimd
        def _(g):
            entry(g, b1)
            for dt in range(NDT):
                if dt % 3 != 2:
                    continue
                if dt >= 3:
                    g.wait_ge(sem["sVE"], b1["sVE"] + (dt - 2))
                    g.wait_ge(sem["sAC"], b1["sAC"] + (dt - 2))
                g.dma_start(xb3[dt % 3][:],
                            xT[dt * 128:(dt + 1) * 128, :]).then_inc(sem[xs3[dt % 3]], 16)

        @b.vector
        def _(v):
            entry(v, b1)
            for dt in range(NDT):
                v.wait_ge(sem[xs3[dt % 3]], b1[xs3[dt % 3]] + 16 * (dt // 3 + 1))
                v.tensor_copy(XNP[:, dt * 1024:(dt + 1) * 1024], xb3[dt % 3][:, 0:1024])
                v.tensor_copy(XNO[:, dt * 1024:(dt + 1) * 1024], xb3[dt % 3][:, 1024:2048]).then_inc(sem["sVE"], 1)

        @b.scalar
        def _(sc):
            entry(sc, b1)
            for dt in range(NDT):
                sc.wait_ge(sem[xs3[dt % 3]], b1[xs3[dt % 3]] + 16 * (dt // 3 + 1))
                if dt >= 2:
                    sc.wait_ge(sem["sPE"], b1["sPE"] + (dt - 1))
                sc.activation(XSQ[dt % 2][:], xb3[dt % 3][:], AF.Square).then_inc(sem["sAC"], 1)

        @b.tensor
        def _(t):
            entry(t, b1)
            for dt in range(NDT):
                t.wait_ge(sem["sVE"], b1["sVE"] + (dt + 1))
                t.wait_ge(sem["sAC"], b1["sAC"] + (dt + 1))
                for c in range(4):
                    src = XNP if c < 2 else XNO
                    cc = dt * 1024 + (c % 2) * 512
                    t.matmul(gS[0:1, c * 512:(c + 1) * 512], ONESBF[:, 0:1], src[:, cc:cc + 512],
                             start=(dt == 0), stop=(dt == NDT - 1))
                last = None
                for c in range(4):
                    last = t.matmul(gQ[0:1, c * 512:(c + 1) * 512], ONESBF[:, 0:1],
                                    XSQ[dt % 2][:, c * 512:(c + 1) * 512],
                                    start=(dt == 0), stop=(dt == NDT - 1))
                last.then_inc(sem["sPE"], 1)
    T["sP0"] += 16 * 11
    T["sP1"] += 16 * 11
    T["sG0"] += 16 * 10
    T["sVE"] += NDT
    T["sAC"] += NDT
    T["sPE"] += NDT

    bq = dict(T)
    with nc.Block() as b:
        @b.vector
        def _(v):
            entry(v, bq)
            inv = 1.0 / D
            chain(v, "v", v.tensor_scalar(QROWS[0:1, 0:2048], gS[0:1, :], inv, None, OP.mult))
            chain(v, "v", v.tensor_scalar(QROWS[32:33, 0:2048], gQ[0:1, :], inv, None, OP.mult))
            chain(v, "v", v.tensor_mul(QROWS[32:33, 2048:4096], QROWS[0:1, 0:2048], QROWS[0:1, 0:2048]))
            chain(v, "v", v.tensor_sub(QROWS[64:65, 0:2048], QROWS[32:33, 0:2048], QROWS[32:33, 2048:4096]))
            v.tensor_scalar(QROWS[64:65, 2048:4096], QROWS[64:65, 0:2048], EPS, None, OP.add).then_inc(sem["sVE"], 1)

        @b.scalar
        def _(sc):
            entry(sc, bq)
            sc.wait_ge(sem["sVE"], bq["sVE"] + 1)
            chain(sc, "s", sc.activation(QROWS[64:65, 0:2048], QROWS[64:65, 2048:4096], AF.Ln))
            sc.activation(QROWS[0:1, 2048:4096], QROWS[64:65, 0:2048], AF.Exp, scale=-0.5).then_inc(sem["sAC"], 1)
    T["sVE"] += 1
    T["sAC"] += 1
    psQ_g.__exit__(None, None, None)
    psS_g.__exit__(None, None, None)

    psBm_g = nc.psum_tensor("psBm", [128, SLOC], F32)
    psBr_g = nc.psum_tensor("psBr", [128, SLOC], F32)
    gBm = psBm_g.__enter__()
    gBr = psBr_g.__enter__()
    bb_ = dict(T)
    with nc.Block() as b:
        @b.tensor
        def _(t):
            entry(t, bb_)
            last = None
            for c in range(4):
                t.matmul(gBm[:, c * 512:(c + 1) * 512], ONESF1[:], QROWS[0:1, c * 512:(c + 1) * 512],
                         start=True, stop=True)
                last = t.matmul(gBr[:, c * 512:(c + 1) * 512], ONESF1[:],
                                QROWS[0:1, 2048 + c * 512:2048 + (c + 1) * 512], start=True, stop=True)
            last.then_inc(sem["sPE"], 1)

        @b.vector
        def _(v):
            entry(v, bb_)
            v.wait_ge(sem["sPE"], bb_["sPE"] + 1)
            v.tensor_copy(MBCB[:], gBm[:])
            v.tensor_copy(RBCB[:], gBr[:]).then_inc(sem["sVE"], 1)
    T["sPE"] += 1
    T["sVE"] += 1
    psBr_g.__exit__(None, None, None)
    psBm_g.__exit__(None, None, None)

    if upto < 2:
        return finish()
    psA_g = nc.psum_tensor("psA", [40, D], F32)
    gA = psA_g.__enter__()
    b2 = dict(T)
    with nc.Block() as b:
        @b.vector
        def _(v):
            entry(v, b2)
            for dt in range(NDT):
                v.tensor_sub(TMPB[:, 0:1024], XNP[:, dt * 1024:(dt + 1) * 1024], MBCB[:, 0:1024])
                chain(v, "v", v.tensor_sub(TMPB[:, 1024:2048], XNO[:, dt * 1024:(dt + 1) * 1024], MBCB[:, 1024:2048]))
                v.tensor_mul(XNP[:, dt * 1024:(dt + 1) * 1024], TMPB[:, 0:1024], RBCB[:, 0:1024])
                v.tensor_mul(XNO[:, dt * 1024:(dt + 1) * 1024], TMPB[:, 1024:2048], RBCB[:, 1024:2048]).then_inc(sem["sVE"], 3)
            v.wait_ge(sem["sCHK"], b2["sCHK"] + 32)
            v.tensor_copy(A01T[0:40, :], gA[0:40, :]).then_inc(sem["sVE"], 1)

        @b.scalar
        def _(sc):
            entry(sc, b2)
            for p in range(32):
                if p >= 3:
                    sc.wait_ge(sem["sCHK"], b2["sCHK"] + (p - 2))
                sc.dma_start(TTS2[p % 3][:],
                             tth[p * 128:(p + 1) * 128, :]).then_inc(sem["sW%d" % (p % 3)], 16)

        @b.tensor
        def _(t):
            entry(t, b2)
            for p in range(32):
                t.wait_ge(sem["sW%d" % (p % 3)], b2["sW%d" % (p % 3)] + 16 * (p // 3 + 1))
                last = None
                for c in range(8):
                    last = t.matmul(gA[0:40, c * 512:(c + 1) * 512], W01S[:, p * 40:(p + 1) * 40],
                                    TTS2[p % 3][:, c * 512:(c + 1) * 512],
                                    start=(p == 0), stop=(p == 31))
                last.then_inc(sem["sCHK"], 1)
    T["sVE"] += 3 * NDT + 1
    T["sW0"] += 16 * 11
    T["sW1"] += 16 * 11
    T["sW2"] += 16 * 10
    T["sCHK"] += 32
    psA_g.__exit__(None, None, None)

    if upto < 3:
        return finish()
    psE_g = nc.psum_tensor("psE", [40, 1024], F32)
    gE = psE_g.__enter__()
    b3a = dict(T)
    with nc.Block() as b:
        @b.tensor
        def _(t):
            entry(t, b3a)
            last = None
            for dt in range(NDT):
                for c in range(2):
                    last = t.matmul(gE[0:40, c * 512:(c + 1) * 512], ENC01[:, dt * 40:(dt + 1) * 40],
                                    XNO[:, dt * 1024 + c * 512:dt * 1024 + (c + 1) * 512],
                                    start=(dt == 0), stop=(dt == NDT - 1))
            last.then_inc(sem["sPEG"], 1)

        @b.scalar
        def _(sc):
            entry(sc, b3a)
            sc.wait_ge(sem["sPEG"], b3a["sPEG"] + 1)
            sc.copy(E01T[0:40, :], gE[0:40, :]).then_inc(sem["sAC"], 1)
    T["sPEG"] += 1
    T["sAC"] += 1
    psE_g.__exit__(None, None, None)

    psK_g = [nc.psum_tensor(f"psK{i}", [128, 1024], F32) for i in range(4)]
    psK = [g.__enter__() for g in psK_g]
    b3 = dict(T)
    with nc.Block() as b:
        @b.gpsimd
        def _(g):
            entry(g, b3)
            g.dma_start(COSD[:], ropec).then_inc(sem["sT2"], 16)
            g.dma_start(SINSGN[:], ropes).then_inc(sem["sT2"], 16)
            wchr = [WCH[0], WCH[1], PB, TMS]
            for i in range(128):
                g2, rp = i // 16, i % 16
                grp, otp = g2 // 2, g2 % 2
                og = grp % 2
                if i >= 4:
                    g.wait_ge(sem["sCHK"], b3["sCHK"] + (i - 3))
                g.dma_start(wchr[i % 4][:, 0:512].rearrange("p (b c) -> p b c", c=256),
                            wk[rp * 256:(rp + 1) * 256,
                               og * 512 + otp * 256:og * 512 + (otp + 1) * 256].rearrange(
                                "(b p) c -> p b c", p=128)).then_inc(sem["sW%d" % (i % 4)], 16)

        @b.tensor
        def _(t):
            entry(t, b3)
            wchr = [WCH[0], WCH[1], PB, TMS]
            for g2 in range(8):
                grp, otp = g2 // 2, g2 % 2
                sh = grp // 2
                pd = (g2 % 2) * 2
                if g2 >= 2:
                    t.wait_ge(sem["sAC"], b3["sAC"] + 2 * (g2 - 1))
                src = XNP if sh == 0 else XNO
                for rp in range(16):
                    i = g2 * 16 + rp
                    t.wait_ge(sem["sW%d" % (i % 4)], b3["sW%d" % (i % 4)] + 16 * (i // 4 + 1))
                    last = None
                    for bb2 in range(2):
                        dt = 2 * rp + bb2
                        for oti in range(2):
                            for c in range(2):
                                last = t.matmul(psK[pd + oti][:, c * 512:(c + 1) * 512],
                                                wchr[i % 4][:, bb2 * 256 + oti * 128:bb2 * 256 + (oti + 1) * 128],
                                                src[:, dt * 1024 + c * 512:dt * 1024 + (c + 1) * 512],
                                                start=(dt == 0), stop=(dt == NDT - 1))
                    last.then_inc(sem["sCHK"], 1)

        @b.scalar
        def _(sc):
            entry(sc, b3)
            for k in range(16):
                g2, oti = k // 2, k % 2
                ps = psK[(g2 % 2) * 2 + oti]
                sc.wait_ge(sem["sCHK"], b3["sCHK"] + 16 * (g2 + 1))
                if k >= 2:
                    sc.wait_ge(sem["sVE"], b3["sVE"] + 3 * (k - 1))
                pbk = PB2 if k % 2 == 0 else PB3
                tmk = TMS2 if k % 2 == 0 else TMS3
                sc.copy(pbk[:], ps[:])
                sc.copy(tmk[0:64, :], ps[64:128, :])
                sc.copy(tmk[64:128, :], ps[0:64, :]).then_inc(sem["sAC"], 1)

        @b.vector
        def _(v):
            entry(v, b3)
            v.wait_ge(sem["sT2"], b3["sT2"] + 32)
            for k in range(16):
                g2, oti = k // 2, k % 2
                grp, otp = g2 // 2, g2 % 2
                sh, og = grp // 2, grp % 2
                h = og * 4 + otp * 2 + oti
                v.wait_ge(sem["sAC"], b3["sAC"] + (k + 1))
                pbk = PB2 if k % 2 == 0 else PB3
                tmk = TMS2 if k % 2 == 0 else TMS3
                v.tensor_mul(PROD[:], pbk[:], COSD[:, sh * 1024:(sh + 1) * 1024])
                chain(v, "v", v.tensor_mul(PROD2[:], tmk[:], SINSGN[:, sh * 1024:(sh + 1) * 1024]))
                v.tensor_add(KT[:, h * 2048 + sh * 1024:h * 2048 + (sh + 1) * 1024],
                             PROD[:], PROD2[:]).then_inc(sem["sVE"], 3)
    T["sT2"] += 32
    for _i in range(4):
        T["sW%d" % _i] += 16 * 32
    T["sCHK"] += 128
    T["sAC"] += 16
    T["sVE"] += 48

    if upto < 4:
        return finish()
    b4 = dict(T)
    with nc.Block() as b:
        @b.gpsimd
        def _(g):
            entry(g, b4)
            wchr = [WCH[0], WCH[1], PB, TMS]
            for i in range(128):
                if i >= 4:
                    g.wait_ge(sem["sCHK"], b4["sCHK"] + (i - 3))
                dt = i % 32
                g.dma_start(wchr[i % 4][:],
                            wv[dt * 128:(dt + 1) * 128, :]).then_inc(sem["sW%d" % (i % 4)], 16)

        @b.tensor
        def _(t):
            entry(t, b4)
            for grp in range(4):
                if grp >= 1:
                    t.wait_ge(sem["sVE"], b4["sVE"] + 4 * grp)
                wchr = [WCH[0], WCH[1], PB, TMS]
                for dt in range(NDT):
                    i = grp * 32 + dt
                    t.wait_ge(sem["sW%d" % (i % 4)], b4["sW%d" % (i % 4)] + 16 * (i // 4 + 1))
                    last = None
                    for sti in range(4):
                        st = grp * 4 + sti
                        src = XNP if st < 8 else XNO
                        col = dt * 1024 + (st % 8) * 128
                        for c in range(2):
                            last = t.matmul(psK[sti][:, c * 512:(c + 1) * 512],
                                            src[:, col:col + 128],
                                            wchr[i % 4][:, c * 512:(c + 1) * 512],
                                            start=(dt == 0), stop=(dt == NDT - 1))
                    last.then_inc(sem["sCHK"], 1)

        @b.vector
        def _(v):
            entry(v, b4)
            for st in range(16):
                v.wait_ge(sem["sCHK"], b4["sCHK"] + 32 * (st // 4 + 1))
                if st >= 2:
                    v.wait_ge(sem["sSC"], b4["sSC"] + 16 * (st - 1))
                buf = PB2 if st % 2 == 0 else TMS2
                v.tensor_copy(buf[:], psK[st % 4][:]).then_inc(sem["sVE"], 1)

        @b.scalar
        def _(sc):
            entry(sc, b4)
            for st in range(16):
                sc.wait_ge(sem["sVE"], b4["sVE"] + st + 1)
                buf = PB2 if st % 2 == 0 else TMS2
                sc.dma_start(vsp[st * 128:(st + 1) * 128, :], buf[:]).then_inc(sem["sSC"], 16)
            sc.wait_ge(sem["sSC"], b4["sSC"] + 16 * 16)
    for _i in range(4):
        T["sW%d" % _i] += 16 * 32
    T["sCHK"] += 128
    T["sVE"] += 16
    T["sSC"] += 16 * 16

    if upto < 5:
        return finish()
    b5 = dict(T)
    with nc.Block() as b:
        @b.gpsimd
        def _(g):
            entry(g, b5)
            wchr = [WCH[0], WCH[1], PB, TMS]
            for i in range(256):
                if i >= 4:
                    g.wait_ge(sem["sCHK"], b5["sCHK"] + (i - 3))
                og2, rp = i // 16, i % 16
                og, otp = og2 // 2, og2 % 2
                g.dma_start(wchr[i % 4][:, 0:512].rearrange("p (b c) -> p b c", c=256),
                            wq[rp * 256:(rp + 1) * 256,
                               og * 512 + otp * 256:og * 512 + (otp + 1) * 256].rearrange(
                                "(b p) c -> p b c", p=128)).then_inc(sem["sW%d" % (i % 4)], 16)

        @b.tensor
        def _(t):
            entry(t, b5)
            wchr = [WCH[0], WCH[1], PB, TMS]
            for og2 in range(16):
                pd = (og2 % 2) * 2
                if og2 >= 2:
                    t.wait_ge(sem["sAC"], b5["sAC"] + 2 * (og2 - 1))
                for rp in range(16):
                    i = og2 * 16 + rp
                    t.wait_ge(sem["sW%d" % (i % 4)], b5["sW%d" % (i % 4)] + 16 * (i // 4 + 1))
                    last = None
                    for bb2 in range(2):
                      dt = 2 * rp + bb2
                      for oti in range(2):
                        for c in range(2):
                            last = t.matmul(psK[pd + oti][:, c * 512:(c + 1) * 512],
                                            wchr[i % 4][:, bb2 * 256 + oti * 128:bb2 * 256 + (oti + 1) * 128],
                                            XNO[:, dt * 1024 + c * 512:dt * 1024 + (c + 1) * 512],
                                            start=(dt == 0), stop=(dt == NDT - 1))
                    last.then_inc(sem["sCHK"], 1)

        @b.scalar
        def _(sc):
            entry(sc, b5)
            for k in range(32):
                og2, oti = k // 2, k % 2
                ps = psK[(og2 % 2) * 2 + oti]
                sc.wait_ge(sem["sCHK"], b5["sCHK"] + 16 * (og2 + 1))
                if k >= 2:
                    sc.wait_ge(sem["sVE"], b5["sVE"] + 3 * (k - 1))
                pbk = PB2 if k % 2 == 0 else PB3
                tmk = TMS2 if k % 2 == 0 else TMS3
                sc.copy(pbk[:], ps[:])
                sc.copy(tmk[0:64, :], ps[64:128, :])
                sc.copy(tmk[64:128, :], ps[0:64, :]).then_inc(sem["sAC"], 1)

        @b.vector
        def _(v):
            entry(v, b5)
            for k in range(32):
                hg = k
                v.wait_ge(sem["sAC"], b5["sAC"] + (k + 1))
                pbk = PB2 if k % 2 == 0 else PB3
                tmk = TMS2 if k % 2 == 0 else TMS3
                v.tensor_mul(PROD[:], pbk[:], COSD[:, 1024:2048])
                chain(v, "v", v.tensor_mul(PROD2[:], tmk[:], SINSGN[:, 1024:2048]))
                v.tensor_add(XNP[:, hg * 1024:(hg + 1) * 1024], PROD[:], PROD2[:]).then_inc(sem["sVE"], 3)
    for _i in range(4):
        T["sW%d" % _i] += 16 * 64
    T["sCHK"] += 256
    T["sAC"] += 32
    T["sVE"] += 96

    if upto < 6:
        return finish()
    b6 = dict(T)
    with nc.Block() as b:
        @b.gpsimd
        def _(g):
            entry(g, b6)
            wchr = [WCH[0], WCH[1], PB, TMS]
            for i in range(256):
                if i >= 4:
                    g.wait_ge(sem["sCHK"], b6["sCHK"] + (i - 3))
                og2, rp = i // 16, i % 16
                og, otp = og2 // 2, og2 % 2
                g.dma_start(wchr[i % 4][:, 0:512].rearrange("p (b c) -> p b c", c=256),
                            wbT[rp * 256:(rp + 1) * 256,
                                og * 512 + otp * 256:og * 512 + (otp + 1) * 256].rearrange(
                                "(b p) c -> p b c", p=128)).then_inc(sem["sW%d" % (i % 4)], 16)

        @b.tensor
        def _(t):
            entry(t, b6)
            wchr = [WCH[0], WCH[1], PB, TMS]
            for og2 in range(16):
                pd = (og2 % 2) * 2
                if og2 >= 2:
                    t.wait_ge(sem["sAC"], b6["sAC"] + 2 * (og2 - 1))
                for rp in range(16):
                    i = og2 * 16 + rp
                    t.wait_ge(sem["sW%d" % (i % 4)], b6["sW%d" % (i % 4)] + 16 * (i // 4 + 1))
                    last = None
                    for bb2 in range(2):
                      dt = 2 * rp + bb2
                      for oti in range(2):
                        for c in range(2):
                            last = t.matmul(psK[pd + oti][:, c * 512:(c + 1) * 512],
                                            wchr[i % 4][:, bb2 * 256 + oti * 128:bb2 * 256 + (oti + 1) * 128],
                                            XNO[:, dt * 1024 + c * 512:dt * 1024 + (c + 1) * 512],
                                            start=(dt == 0), stop=False)
                    last.then_inc(sem["sCHK"], 1)
                for oti in range(2):
                    k = og2 * 2 + oti
                    last = None
                    for c in range(2):
                        last = t.matmul(psK[pd + oti][:, c * 512:(c + 1) * 512],
                                        A01T[0:40, k * 128:(k + 1) * 128],
                                        E01T[0:40, c * 512:(c + 1) * 512],
                                        start=False, stop=True)
                    last.then_inc(sem["sPE"], 1)

        @b.scalar
        def _(sc):
            entry(sc, b6)
            for k in range(32):
                og2, oti = k // 2, k % 2
                ps = psK[(og2 % 2) * 2 + oti]
                sc.wait_ge(sem["sPE"], b6["sPE"] + k + 1)
                if k >= 2:
                    sc.wait_ge(sem["sG%d" % (k % 2)], b6["sG%d" % (k % 2)] + 16 * (k // 2))
                sc.activation(G[k % 2][:], ps[:], AF.Silu, bias=BB[:, k:k + 1]).then_inc(sem["sAC"], 1)
                sc.wait_ge(sem["sAC"], b6["sAC"] + k + 1)
                sc.dma_start(gsp[k * 128:(k + 1) * 128, :], G[k % 2][:]).then_inc(sem["sG%d" % (k % 2)], 16)
            sc.wait_ge(sem["sG0"], b6["sG0"] + 16 * 16)
            sc.wait_ge(sem["sG1"], b6["sG1"] + 16 * 16)
    for _i in range(4):
        T["sW%d" % _i] += 16 * 64
    T["sCHK"] += 256
    T["sPE"] += 32
    T["sAC"] += 32
    T["sG0"] += 16 * 16
    T["sG1"] += 16 * 16
    for g in reversed(psK_g):
        g.__exit__(None, None, None)

    if upto < 7:
        return finish()
    scp_g = [nc.psum_tensor(f"scp{i}", [128, 1024], F32) for i in range(2)]
    scp = [g.__enter__() for g in scp_g]
    sums_g = [nc.psum_tensor("sums0", [128, 512], F32)]
    sums_ps = [g.__enter__() for g in sums_g]
    av_g = [nc.psum_tensor(f"av{i}", [128, 512], F32) for i in range(3)]
    av_ps = [g.__enter__() for g in av_g]
    units = [(h, g, qc) for h in range(8) for g in range(4) for qc in range(2)]
    units = units[:nunits]
    NU = len(units)
    b7 = dict(T)

    def attn_pair(seti, jp):
        if seti == 2:
            if jp < 4:
                return AT2A[:, jp * 1024:(jp + 1) * 1024]
            return XNO[:, 30720 + (jp - 4) * 1024:30720 + (jp - 3) * 1024]
        col = 16384 + (seti * 12 + 2 * jp) * 512
        return XNO[:, col:col + 1024]

    def exptmp_pair(i):
        col = 16384 + 24 * 512 + i * 1024
        return XNO[:, col:col + 1024]

    ORD = [0, 1, 4, 5, 2, 3]

    def qlo(j):
        return 128 * (j - 8) if j >= 8 else 0

    def qhi(j):
        return min(512, 128 * (j + 1)) if j < 4 else 512

    def unit_scores(t, u):
        h, g_, qc = units[u]
        hg = h * 4 + g_
        for pos, jp in enumerate(ORD):
            tgt = 6 * u + pos - 1
            if tgt > 0:
                t.wait_ge(sem["sEXP"], b7["sEXP"] + tgt)
            for jj in range(2):
                j = 2 * jp + jj
                st = 4 * qc + j
                lo, hi = qlo(j), qhi(j)
                t.matmul(scp[pos % 2][:, jj * 512 + lo:jj * 512 + hi],
                         KT[:, h * 2048 + st * 128:h * 2048 + (st + 1) * 128],
                         XNP[:, hg * 1024 + qc * 512 + lo:hg * 1024 + qc * 512 + hi],
                         start=True, stop=True).then_inc(sem["sSCR"], 1)

    with nc.Block() as b:
        @b.sync
        def _(s):
            entry(s, b7)
            for st in range(16):
                s.dma_start(XNO[:, st * 1024:(st + 1) * 1024],
                            vsp[st * 128:(st + 1) * 128, :]).then_inc(sem["sIN"], 16)

        @b.gpsimd
        def _(g):
            entry(g, b7)
            for j in range(8):
                g.dma_start(MASKS[:, j * 512:(j + 1) * 512],
                            masks8[j * 128:(j + 1) * 128, :]).then_inc(sem["sGP"], 16)
            g.dma_start(CORRS[:], corr).then_inc(sem["sGP"], 16)

        @b.tensor
        def _(t):
            entry(t, b7)
            if NU > 0:
                unit_scores(t, 0)
            for u in range(NU):
                h, g_, qc = units[u]
                if u + 1 < NU:
                    unit_scores(t, u + 1)
                t.wait_ge(sem["sEXP"], b7["sEXP"] + 6 * u + 6)
                t.wait_ge(sem["sMS"], b7["sMS"] + 4 * u + 4)
                if u >= 1:
                    t.wait_ge(sem["sCB"], b7["sCB"] + u)
                last = None
                for j in range(12):
                    lo, hi = qlo(j), qhi(j)
                    last = t.matmul(sums_ps[0][:, lo:hi], ONESBF[:],
                                    attn_tile(u % 3, j)[:, lo:hi],
                                    start=(j == 0), stop=(j == 11))
                last.then_inc(sem["sSUM"], 1)
                if u == 0:
                    t.wait_ge(sem["sIN"], b7["sIN"] + 16 * 16)
                if u >= 3:
                    t.wait_ge(sem["sOE"], b7["sOE"] + u - 2)
                last = None
                for j in range(12):
                    st = 4 * qc + j
                    lo, hi = qlo(j), qhi(j)
                    last = t.matmul(av_ps[u % 3][:, lo:hi],
                                    XNO[:, st * 1024 + h * 128:st * 1024 + (h + 1) * 128],
                                    attn_tile(u % 3, j)[:, lo:hi], start=(j == 0), stop=(j == 11))
                last.then_inc(sem["sAV"], 1)

        @b.scalar
        def _(sc):
            entry(sc, b7)
            def recip(u):
                sc.wait_ge(sem["sCB"], b7["sCB"] + u + 1)
                if u >= 2:
                    sc.wait_ge(sem["sOE"], b7["sOE"] + u - 1)
                chain(sc, "s", sc.activation(LNT[:], SUMS[u % 2][:], AF.Ln))
                sc.activation(RECIP[u % 2][:], LNT[:], AF.Exp, scale=-1.0).then_inc(sem["sRC"], 1)
            for u in range(NU):
                for pos, jp in enumerate(ORD):
                    sc.wait_ge(sem["sSCR"], b7["sSCR"] + 12 * u + 2 * pos + 2)
                    if pos == 0 and u >= 3:
                        sc.wait_ge(sem["sAV"], b7["sAV"] + u - 2)
                    if jp < 2:
                        tgt = 4 * u + jp - 1
                        if tgt > 0:
                            sc.wait_ge(sem["sMS"], b7["sMS"] + tgt)
                        dst = exptmp_pair(jp)
                    elif jp >= 4:
                        sc.wait_ge(sem["sMS"], b7["sMS"] + 4 * u + jp - 3)
                        dst = exptmp_pair(jp - 4)
                    else:
                        dst = attn_pair(u % 3, jp)
                    sc.activation(dst, scp[pos % 2][:], AF.Exp, scale=SM).then_inc(sem["sEXP"], 1)
                if u >= 1:
                    recip(u - 1)
            if NU > 0:
                recip(NU - 1)

        @b.vector
        def _(v):
            entry(v, b7)
            v.wait_ge(sem["sGP"], b7["sGP"] + 16 * 9)

            def vmasks(u):
                for mp in range(4):
                    jp = mp if mp < 2 else mp + 2
                    v.wait_ge(sem["sEXP"], b7["sEXP"] + 6 * u + mp + 1)
                    if mp == 0 and u >= 3:
                        v.wait_ge(sem["sAV"], b7["sAV"] + u - 2)
                    v.tensor_mul(attn_pair(u % 3, jp), exptmp_pair(jp % 2 if jp < 2 else jp - 4),
                                 MASKS[:, mp * 1024:(mp + 1) * 1024]).then_inc(sem["sMS"], 1)

            if NU > 0:
                vmasks(0)
            for u in range(NU):
                h, g_, qc = units[u]
                if u + 1 < NU:
                    vmasks(u + 1)
                v.wait_ge(sem["sSUM"], b7["sSUM"] + u + 1)
                if u >= 2:
                    v.wait_ge(sem["sRC"], b7["sRC"] + u - 1)
                v.tensor_sub(SUMS[u % 2][:], sums_ps[0][:],
                             CORRS[:, qc * 512:(qc + 1) * 512]).then_inc(sem["sCB"], 1)
                v.wait_ge(sem["sAV"], b7["sAV"] + u + 1)
                v.wait_ge(sem["sRC"], b7["sRC"] + u + 1)
                hg = h * 4 + g_
                v.tensor_mul(XNP[:, hg * 1024 + qc * 512:hg * 1024 + (qc + 1) * 512],
                             av_ps[u % 3][:], RECIP[u % 2][:]).then_inc(sem["sOE"], 1)
    T["sIN"] += 16 * 16
    T["sGP"] += 16 * 9
    T["sSCR"] += 12 * NU
    T["sEXP"] += 6 * NU
    T["sMS"] += 4 * NU
    T["sSUM"] += NU
    T["sAV"] += NU
    T["sCB"] += NU
    T["sRC"] += NU
    T["sOE"] += NU
    for g in reversed(scp_g + sums_g + av_g):
        g.__exit__(None, None, None)

    if upto < 8:
        return finish()
    psO_g = [nc.psum_tensor(f"psO{i}", [128, 1024], F32) for i in range(4)]
    psO = [g.__enter__() for g in psO_g]
    b8 = dict(T)
    with nc.Block() as b:
        @b.gpsimd
        def _(g):
            entry(g, b8)
            wchr = [WCH[0], WCH[1], PB, TMS]
            for i in range(256):
                if i >= 4:
                    g.wait_ge(sem["sCHK"], b8["sCHK"] + (i - 3))
                og2, rp = i // 16, i % 16
                og, otp = og2 // 2, og2 % 2
                g.dma_start(wchr[i % 4][:, 0:512].rearrange("p (b c) -> p b c", c=256),
                            wo[rp * 256:(rp + 1) * 256,
                               og * 512 + otp * 256:og * 512 + (otp + 1) * 256].rearrange(
                                "(b p) c -> p b c", p=128)).then_inc(sem["sW%d" % (i % 4)], 16)

        @b.scalar
        def _(sc):
            entry(sc, b8)
            for k in range(32):
                if k >= 2:
                    sc.wait_ge(sem["sVE"], b8["sVE"] + 2 * (k - 2) + 2)
                sc.dma_start(GTB[k % 2][:], gsp[k * 128:(k + 1) * 128, :]).then_inc(sem["sP%d" % (k % 2)], 16)
                sc.dma_start(XTT[k % 2][:], xT[k * 128:(k + 1) * 128, 1024:2048]).then_inc(sem["sP%d" % (k % 2)], 16)

        @b.tensor
        def _(t):
            entry(t, b8)
            wchr = [WCH[0], WCH[1], PB, TMS]
            for og2 in range(16):
                pd = (og2 % 2) * 2
                if og2 >= 2:
                    t.wait_ge(sem["sVE"], b8["sVE"] + 4 * (og2 - 1))
                for rp in range(16):
                    i = og2 * 16 + rp
                    t.wait_ge(sem["sW%d" % (i % 4)], b8["sW%d" % (i % 4)] + 16 * (i // 4 + 1))
                    last = None
                    for bb2 in range(2):
                      ht = 2 * rp + bb2
                      for oti in range(2):
                        for c in range(2):
                            last = t.matmul(psO[pd + oti][:, c * 512:(c + 1) * 512],
                                            wchr[i % 4][:, bb2 * 256 + oti * 128:bb2 * 256 + (oti + 1) * 128],
                                            XNP[:, ht * 1024 + c * 512:ht * 1024 + (c + 1) * 512],
                                            start=(ht == 0), stop=(ht == NDT - 1))
                    last.then_inc(sem["sCHK"], 1)

        @b.vector
        def _(v):
            entry(v, b8)
            for k in range(32):
                og2, oti = k // 2, k % 2
                v.wait_ge(sem["sCHK"], b8["sCHK"] + 16 * (og2 + 1))
                v.wait_ge(sem["sP%d" % (k % 2)], b8["sP%d" % (k % 2)] + 32 * (k // 2 + 1))
                if k >= 4:
                    v.wait_ge(sem["sG%d" % (k % 2)], b8["sG%d" % (k % 2)] + 16 * (k // 2 - 1))
                pso = psO[(og2 % 2) * 2 + oti]
                chain(v, "v", v.tensor_mul(OUTA[k % 2][:], pso[:], GTB[k % 2][:]))
                v.tensor_add(OUTB[k % 4][:], OUTA[k % 2][:], XTT[k % 2][:]).then_inc(sem["sVE"], 2)

        @b.sync
        def _(s):
            entry(s, b8)
            for k in range(32):
                s.wait_ge(sem["sVE"], b8["sVE"] + 2 * (k + 1))
                s.dma_start(out_t[k * 128:(k + 1) * 128, :], OUTB[k % 4][:]).then_inc(sem["sG%d" % (k % 2)], 16)
            s.wait_ge(sem["sG0"], b8["sG0"] + 16 * 16)
            s.wait_ge(sem["sG1"], b8["sG1"] + 16 * 16)
    for _i in range(4):
        T["sW%d" % _i] += 16 * 64
    T["sCHK"] += 256
    T["sP0"] += 16 * 32
    T["sP1"] += 16 * 32
    T["sVE"] += 64
    T["sG0"] += 16 * 16
    T["sG1"] += 16 * 16
    for g in reversed(psO_g):
        g.__exit__(None, None, None)

    return finish()


_CACHE = {}


def prep_in_maps(L0_attn_x, L0_attn_wq, L0_attn_wk, L0_attn_wv, L0_attn_wo,
                 L0_ffn_W_base, L0_ffn_b_base, L0_ffn_enc0, L0_ffn_enc1,
                 L0_ffn_coefs0, L0_ffn_coefs1, L0_ffn_mask0, L0_ffn_mask1):
    x2d = np.asarray(L0_attn_x, dtype=np.float32).reshape(S, D)

    wq = np.ascontiguousarray(np.asarray(L0_attn_wq, np.float32))
    wk = np.ascontiguousarray(np.asarray(L0_attn_wk, np.float32))
    wv = np.ascontiguousarray(np.asarray(L0_attn_wv, np.float32))
    wo = np.ascontiguousarray(np.asarray(L0_attn_wo, np.float32))
    wbT = np.ascontiguousarray(np.asarray(L0_ffn_W_base, np.float32).T)
    bbt = np.ascontiguousarray(np.asarray(L0_ffn_b_base, np.float32).reshape(32, 128).T)
    enc01T = np.ascontiguousarray(np.concatenate(
        [np.asarray(L0_ffn_enc0, np.float32), np.asarray(L0_ffn_enc1, np.float32)], axis=0).T)
    c0 = np.asarray(L0_ffn_coefs0, np.float32)
    c1 = np.asarray(L0_ffn_coefs1, np.float32)
    m0 = np.asarray(L0_ffn_mask0).astype(np.uint8)
    m1 = np.asarray(L0_ffn_mask1).astype(np.uint8)
    coefs01 = np.ascontiguousarray(np.concatenate(
        [c0[0], c1[0], c0[1], c1[1], c0[2], c1[2]], axis=1))
    mask01 = np.ascontiguousarray(np.concatenate(
        [m0[0], m1[0], m0[1], m1[1], m0[2], m1[2]], axis=1))

    tth = _tth()
    masks8 = _masks8()

    in_maps = []
    for core in range(NCORE):
        lo = core * OWN
        xTc = np.zeros((SLOC, D), np.float32)
        s0 = max(0, lo - 1024)
        xTc[SLOC - (lo + 1024 - s0):] = x2d[s0:lo + 1024]
        xTc = np.ascontiguousarray(xTc.T)
        cosd, sinsgn = _rope_tables(core)
        in_maps.append({
            "xT": xTc, "wq": wq, "wk": wk, "wv": wv, "wo": wo, "wbT": wbT,
            "bbt": bbt, "enc01T": enc01T, "coefs01": coefs01, "mask01": mask01,
            "tth": tth, "ropec": cosd, "ropes": sinsgn, "masks8": masks8,
            "corr": _corr(core),
        })
    return in_maps


def kernel(**inputs):
    in_maps = prep_in_maps(**inputs)
    if "nc" not in _CACHE:
        _CACHE["nc"] = build_nc()
    nc, _stack = _CACHE["nc"]
    res = run_bass_kernel_spmd(nc, in_maps, core_ids=list(range(NCORE)))
    out = np.concatenate([np.ascontiguousarray(r["outT"].T) for r in res.results], axis=0)
    return out.reshape(1, S, D)


# version 17 (final)
# speedup vs baseline: 1.2016x; 1.0684x over previous
import math
import contextlib
import numpy as np
import ml_dtypes

import concourse.bass as bass
import concourse.mybir as mybir
from concourse.bass_utils import run_bass_kernel_spmd

F32 = mybir.dt.float32
BF16 = mybir.dt.bfloat16
U8 = mybir.dt.uint8
AF = mybir.ActivationFunctionType
OP = mybir.AluOpType
BF = ml_dtypes.bfloat16

D = 4096
S = 8192
NCORE = 8
OWN = 1024
SLOC = 2048
NDT = 32
SM = 1.0 / math.sqrt(128.0)
EPS = 1e-5



def _tth():
    n = D
    sqrt_n = math.sqrt(n)
    pair_scale = math.sqrt(n / 2.0)
    max_f = (n - 1) // 2
    eye = np.eye(n, dtype=np.float64)
    dc = (eye[:, 0] * sqrt_n)[:, None].astype(np.complex128)
    mid = eye[:, 1:1 + 2 * max_f:2] * pair_scale - 1j * (eye[:, 2:2 + 2 * max_f:2] * pair_scale)
    ny = (eye[:, -1] * sqrt_n)[:, None].astype(np.complex128)
    spectrum = np.concatenate([dc, mid, ny], axis=1)
    T = np.fft.irfft(spectrum, n=n, axis=1).T
    return np.ascontiguousarray((0.5 * T).T).astype(BF)


def _masks8():
    s = np.arange(128)[:, None]
    q = np.arange(512)[None, :]
    out = np.zeros((8, 128, 512), np.float32)
    for j in range(4):
        out[j] = (128 * j + s > q)
    for j in range(8, 12):
        out[j - 4] = (128 * j + s <= q + 1024)
    return np.ascontiguousarray(out.reshape(8 * 128, 512)).astype(BF)


def _rope_tables(core):
    half = 64
    inv = 10000.0 ** (-np.arange(half, dtype=np.float64) / half)
    pos = np.arange(core * OWN - 1024, core * OWN + 1024, dtype=np.float64)
    ang = inv[:, None] * pos[None, :]
    cos, sin = np.cos(ang), np.sin(ang)
    cosd = np.concatenate([cos, cos], axis=0).astype(BF)
    sinsgn = np.concatenate([-sin, sin], axis=0).astype(BF)
    return np.ascontiguousarray(cosd), np.ascontiguousarray(sinsgn)


def _corr(core):
    c = np.zeros(OWN, np.float32)
    if core == 0:
        c = np.maximum(0, 1023 - np.arange(OWN)).astype(np.float32)
    return np.ascontiguousarray(np.broadcast_to(c[None, :], (128, OWN)))



def build_nc(upto=99, taps=(), nunits=64):
    nc = bass.Bass("TRN2", target_bir_lowering=False)

    xT = nc.dram_tensor("xT", [D, SLOC], F32, kind="ExternalInput").ap()
    wq = nc.dram_tensor("wq", [D, D], F32, kind="ExternalInput").ap()
    wk = nc.dram_tensor("wk", [D, 1024], F32, kind="ExternalInput").ap()
    wv = nc.dram_tensor("wv", [D, 1024], F32, kind="ExternalInput").ap()
    wo = nc.dram_tensor("wo", [D, D], F32, kind="ExternalInput").ap()
    wbT = nc.dram_tensor("wbT", [D, D], F32, kind="ExternalInput").ap()
    bbt = nc.dram_tensor("bbt", [128, 32], F32, kind="ExternalInput").ap()
    enc01T = nc.dram_tensor("enc01T", [D, 40], F32, kind="ExternalInput").ap()
    coefs01 = nc.dram_tensor("coefs01", [D, 120], F32, kind="ExternalInput").ap()
    mask01 = nc.dram_tensor("mask01", [D, 120], U8, kind="ExternalInput").ap()
    tth = nc.dram_tensor("tth", [D, D], BF16, kind="ExternalInput").ap()
    ropec = nc.dram_tensor("ropec", [128, SLOC], BF16, kind="ExternalInput").ap()
    ropes = nc.dram_tensor("ropes", [128, SLOC], BF16, kind="ExternalInput").ap()
    masks8 = nc.dram_tensor("masks8", [8 * 128, 512], BF16, kind="ExternalInput").ap()
    corr = nc.dram_tensor("corr", [128, OWN], F32, kind="ExternalInput").ap()
    vsp = nc.dram_tensor("vsp", [SLOC, 1024], BF16).ap()
    gsp = nc.dram_tensor("gsp", [D, OWN], F32).ap()
    out_t = nc.dram_tensor("outT", [D, OWN], F32, kind="ExternalOutput").ap()

    arena = nc.alloc_sbuf_tensor("arena", [128, 210848], U8)
    BASE = nc.lookup_mloc(arena).addr

    def sbat(name, shape, dtype, off):
        return nc.alloc_sbuf_tensor_at(name, shape, dtype, offset=BASE + off)

    XNP = sbat("XNP", [128, 32768], BF16, 0)
    XNO = sbat("XNO", [128, 32768], BF16, 65536)
    KT = sbat("KT", [128, 16384], BF16, 131072)
    COEFC = sbat("COEFC", [128, 3840], BF16, 131072)
    COEFM = sbat("COEFM", [128, 3840], BF16, 138752)
    XBUF = [sbat(f"XBUF{i}", [128, SLOC], F32, 131072 + 8192 * i) for i in range(2)]
    TTB = [sbat(f"TTB{i}", [128, 2048], BF16, 147456 + 4096 * i) for i in range(2)]
    TTS2 = [sbat("TTS2a", [128, 4096], BF16, 147456), sbat("TTS2b", [128, 4096], BF16, 155648),
            sbat("TTS2c", [128, 4096], BF16, 163840)]
    TMPB = sbat("TMPB", [128, SLOC], BF16, 131072)
    MBCB = sbat("MBCB", [128, SLOC], BF16, 139264)
    RBCB = sbat("RBCB", [128, SLOC], BF16, 143360)
    XBUF2 = sbat("XBUF2", [128, SLOC], F32, 147456)
    XSQ = [sbat(f"XSQ{i}", [128, SLOC], BF16, 155648 + 4096 * i) for i in range(2)]
    COSD = sbat("COSD", [128, SLOC], BF16, 163840)
    SINSGN = sbat("SINSGN", [128, SLOC], BF16, 167936)
    TMPF = sbat("TMPF", [128, SLOC], F32, 163840)
    G = [sbat(f"G{i}", [128, OWN], F32, 163840 + 4096 * i) for i in range(2)]
    A01T = sbat("A01T", [40, D], BF16, 172032)
    MASKS = sbat("MASKS", [128, 4096], BF16, 172032)
    WCH = [sbat(f"WCH{i}", [128, 1024], BF16, 180224 + 2048 * i) for i in range(2)]
    PB = sbat("PB", [128, 1024], BF16, 184320)
    TMS = sbat("TMS", [128, 1024], BF16, 186368)
    QROWS = sbat("QROWS", [128, 4096], F32, 188416)
    MBC = sbat("MBC", [128, SLOC], F32, 188416)
    RBC = sbat("RBC", [128, SLOC], F32, 196608)
    PROD = sbat("PROD", [128, 1024], BF16, 188416)
    PROD2 = sbat("PROD2", [128, 1024], BF16, 190464)
    PB2 = sbat("PB2", [128, 1024], BF16, 194560)
    TMS2 = sbat("TMS2", [128, 1024], BF16, 196608)
    PB3 = sbat("PB3", [128, 1024], BF16, 198656)
    TMS3 = sbat("TMS3", [128, 1024], BF16, 200704)
    E01T = sbat("E01T", [40, 1024], BF16, 192512)
    CORRS = sbat("CORRS", [128, OWN], F32, 188416)
    SUMS = [sbat(f"SUMS{i}", [128, 512], F32, 192512 + 2048 * i) for i in range(2)]
    RECIP = [sbat(f"RECIP{i}", [128, 512], F32, 196608 + 2048 * i) for i in range(2)]
    LNT = sbat("LNT", [128, 512], F32, 200704)
    GTB = [sbat(f"GTB{i}", [128, OWN], F32, 188416 + 4096 * i) for i in range(2)]
    XTT = [sbat(f"XTT{i}", [128, OWN], F32, 196608 + 4096 * i) for i in range(2)]
    W01S = sbat("W01S", [128, 1280], BF16, 204800)
    ENC01 = sbat("ENC01", [128, 1280], BF16, 207360)
    BB = sbat("BB", [128, 32], F32, 209920)
    ONESBF = sbat("ONESBF", [128, 128], BF16, 210048)
    ONESF1 = sbat("ONESF1", [1, 128], F32, 210304)
    EPSC = sbat("EPSC", [1, 1], F32, 210816)
    OUTA = [sbat(f"OUTA{i}", [128, OWN], F32, 65536 + 4096 * i) for i in range(2)]
    OUTB = [sbat(f"OUTB{i}", [128, OWN], F32, 65536 + 8192 + 4096 * i) for i in range(4)]

    def attn_tile(seti, j):
        col = 16384 + (seti * 12 + j) * 512
        return XNO[:, col:col + 512]

    def exptmp_tile(slot):
        col = 16384 + 24 * 512 + slot * 512
        return XNO[:, col:col + 512]

    def finish():
        if taps:
            tsem = stack.enter_context(nc.semaphore("tsem"))
            tmap = {"XNP": (XNP, BF16), "XNO": (XNO, BF16), "KT": (KT, BF16),
                    "VSP": (vsp, BF16), "GSP": (gsp, F32), "A01T": (A01T, BF16),
                    "E01T": (E01T, BF16), "W01S": (W01S, BF16), "COEFC": (COEFC, BF16), "COEFM": (COEFM, BF16)}
            base = dict(T)
            with nc.Block() as b:
                @b.sync
                def _(s):
                    entry(s, base)
                    n = 0
                    for name in taps:
                        t, dt_ = tmap[name]
                        shape = list(t.shape)
                        d = nc.dram_tensor("tap_" + name, shape, dt_,
                                           kind="ExternalOutput").ap()
                        s.dma_start(d, t[:]).then_inc(tsem, 16)
                        n += 16
                    s.wait_ge(tsem, n)
        return nc, stack

    stack = contextlib.ExitStack()
    sem = {}
    for name in ["sIN", "sGP", "sSC", "sCHK", "sPE", "sPEG", "sVE", "sAC",
                 "sSCR", "sEXP", "sMS", "sSUM", "sAV", "sCB", "sRC", "sOE",
                 "sW0", "sW1", "sW2", "sW3", "sP0", "sP1", "sG0", "sG1", "sT2"]:
        sem[name] = stack.enter_context(nc.semaphore(name))
    T = {k: 0 for k in sem}
    ALLSEMS = list(sem.keys())

    def entry(e, base):
        for k in ALLSEMS:
            if base[k] > 0:
                e.wait_ge(sem[k], base[k])

    chsem = {"v": stack.enter_context(nc.semaphore("chv")),
             "s": stack.enter_context(nc.semaphore("chs"))}
    chcnt = {"v": 0, "s": 0}

    def chain(eng, kind, instr):
        chcnt[kind] += 1
        instr.then_inc(chsem[kind], 1)
        eng.wait_ge(chsem[kind], chcnt[kind])

    b0 = dict(T)
    with nc.Block() as b:
        @b.gpsimd
        def _(g):
            g.dma_start(COEFC[:], coefs01.rearrange("(t p) c -> p t c", p=128)).then_inc(sem["sGP"], 16)
            g.dma_start(COEFM[:], mask01.rearrange("(t p) c -> p t c", p=128)).then_inc(sem["sGP"], 16)
            g.dma_start(ENC01[:], enc01T.rearrange("(t p) c -> p t c", p=128)).then_inc(sem["sGP"], 16)
            g.dma_start(BB[:], bbt).then_inc(sem["sGP"], 16)
            g.memset(ONESBF[:], 1.0)
            g.memset(ONESF1[:], 1.0)
            g.memset(EPSC[:], EPS).then_inc(sem["sPEG"], 1)
            g.wait_ge(sem["sGP"], b0["sGP"] + 64)

        @b.vector
        def _(v):
            v.wait_ge(sem["sGP"], b0["sGP"] + 64)
            for t in range(NDT):
                cs = slice(t * 120, (t + 1) * 120)
                chain(v, "v", v.tensor_mul(XSQ[0][:, 0:120], COEFC[:, cs], COEFM[:, cs]))
                chain(v, "v", v.tensor_add(XSQ[0][:, 128:168], XSQ[0][:, 0:40], XSQ[0][:, 40:80]))
                last = v.tensor_add(W01S[:, t * 40:(t + 1) * 40], XSQ[0][:, 128:168], XSQ[0][:, 80:120])
            last.then_inc(sem["sVE"], 1)
    T["sGP"] += 64
    T["sPEG"] += 1
    T["sVE"] += 1

    if upto < 1:
        return finish()
    psS_g = nc.psum_tensor("psS", [1, SLOC], F32)
    psQ_g = nc.psum_tensor("psQ", [1, SLOC], F32)
    gS = psS_g.__enter__()
    gQ = psQ_g.__enter__()
    b1 = dict(T)
    with nc.Block() as b:
        xb3 = [XBUF[0], XBUF[1], XBUF2]
        xs3 = ["sP0", "sP1", "sG0"]

        @b.sync
        def _(s):
            entry(s, b1)
            for dt in range(NDT):
                if dt % 3 == 2:
                    continue
                if dt >= 3:
                    s.wait_ge(sem["sVE"], b1["sVE"] + (dt - 2))
                    s.wait_ge(sem["sAC"], b1["sAC"] + (dt - 2))
                s.dma_start(xb3[dt % 3][:],
                            xT[dt * 128:(dt + 1) * 128, :]).then_inc(sem[xs3[dt % 3]], 16)

        @b.gpsimd
        def _(g):
            entry(g, b1)
            for dt in range(NDT):
                if dt % 3 != 2:
                    continue
                if dt >= 3:
                    g.wait_ge(sem["sVE"], b1["sVE"] + (dt - 2))
                    g.wait_ge(sem["sAC"], b1["sAC"] + (dt - 2))
                g.dma_start(xb3[dt % 3][:],
                            xT[dt * 128:(dt + 1) * 128, :]).then_inc(sem[xs3[dt % 3]], 16)

        @b.vector
        def _(v):
            entry(v, b1)
            for dt in range(NDT):
                v.wait_ge(sem[xs3[dt % 3]], b1[xs3[dt % 3]] + 16 * (dt // 3 + 1))
                v.tensor_copy(XNP[:, dt * 1024:(dt + 1) * 1024], xb3[dt % 3][:, 0:1024])
                v.tensor_copy(XNO[:, dt * 1024:(dt + 1) * 1024], xb3[dt % 3][:, 1024:2048]).then_inc(sem["sVE"], 1)

        @b.scalar
        def _(sc):
            entry(sc, b1)
            for dt in range(NDT):
                sc.wait_ge(sem[xs3[dt % 3]], b1[xs3[dt % 3]] + 16 * (dt // 3 + 1))
                if dt >= 2:
                    sc.wait_ge(sem["sPE"], b1["sPE"] + (dt - 1))
                sc.activation(XSQ[dt % 2][:], xb3[dt % 3][:], AF.Square).then_inc(sem["sAC"], 1)

        @b.tensor
        def _(t):
            entry(t, b1)
            for dt in range(NDT):
                t.wait_ge(sem["sVE"], b1["sVE"] + (dt + 1))
                t.wait_ge(sem["sAC"], b1["sAC"] + (dt + 1))
                for c in range(4):
                    src = XNP if c < 2 else XNO
                    cc = dt * 1024 + (c % 2) * 512
                    t.matmul(gS[0:1, c * 512:(c + 1) * 512], ONESBF[:, 0:1], src[:, cc:cc + 512],
                             start=(dt == 0), stop=(dt == NDT - 1))
                last = None
                for c in range(4):
                    last = t.matmul(gQ[0:1, c * 512:(c + 1) * 512], ONESBF[:, 0:1],
                                    XSQ[dt % 2][:, c * 512:(c + 1) * 512],
                                    start=(dt == 0), stop=(dt == NDT - 1))
                last.then_inc(sem["sPE"], 1)
    T["sP0"] += 16 * 11
    T["sP1"] += 16 * 11
    T["sG0"] += 16 * 10
    T["sVE"] += NDT
    T["sAC"] += NDT
    T["sPE"] += NDT

    bq = dict(T)
    with nc.Block() as b:
        @b.vector
        def _(v):
            entry(v, bq)
            inv = 1.0 / D
            chain(v, "v", v.tensor_scalar(QROWS[0:1, 0:2048], gS[0:1, :], inv, None, OP.mult))
            chain(v, "v", v.tensor_scalar(QROWS[32:33, 0:2048], gQ[0:1, :], inv, None, OP.mult))
            chain(v, "v", v.tensor_mul(QROWS[32:33, 2048:4096], QROWS[0:1, 0:2048], QROWS[0:1, 0:2048]))
            chain(v, "v", v.tensor_sub(QROWS[64:65, 0:2048], QROWS[32:33, 0:2048], QROWS[32:33, 2048:4096]))
            v.tensor_scalar(QROWS[64:65, 2048:4096], QROWS[64:65, 0:2048], EPS, None, OP.add).then_inc(sem["sVE"], 1)

        @b.scalar
        def _(sc):
            entry(sc, bq)
            sc.wait_ge(sem["sVE"], bq["sVE"] + 1)
            chain(sc, "s", sc.activation(QROWS[64:65, 0:2048], QROWS[64:65, 2048:4096], AF.Ln))
            sc.activation(QROWS[0:1, 2048:4096], QROWS[64:65, 0:2048], AF.Exp, scale=-0.5).then_inc(sem["sAC"], 1)
    T["sVE"] += 1
    T["sAC"] += 1
    psQ_g.__exit__(None, None, None)
    psS_g.__exit__(None, None, None)

    psBm_g = nc.psum_tensor("psBm", [128, SLOC], F32)
    psBr_g = nc.psum_tensor("psBr", [128, SLOC], F32)
    gBm = psBm_g.__enter__()
    gBr = psBr_g.__enter__()
    bb_ = dict(T)
    with nc.Block() as b:
        @b.tensor
        def _(t):
            entry(t, bb_)
            last = None
            for c in range(4):
                t.matmul(gBm[:, c * 512:(c + 1) * 512], ONESF1[:], QROWS[0:1, c * 512:(c + 1) * 512],
                         start=True, stop=True)
                last = t.matmul(gBr[:, c * 512:(c + 1) * 512], ONESF1[:],
                                QROWS[0:1, 2048 + c * 512:2048 + (c + 1) * 512], start=True, stop=True)
            last.then_inc(sem["sPE"], 1)

        @b.vector
        def _(v):
            entry(v, bb_)
            v.wait_ge(sem["sPE"], bb_["sPE"] + 1)
            v.tensor_copy(MBCB[:], gBm[:])
            v.tensor_copy(RBCB[:], gBr[:]).then_inc(sem["sVE"], 1)
    T["sPE"] += 1
    T["sVE"] += 1
    psBr_g.__exit__(None, None, None)
    psBm_g.__exit__(None, None, None)

    if upto < 2:
        return finish()
    psA_g = nc.psum_tensor("psA", [40, D], F32)
    gA = psA_g.__enter__()
    b2 = dict(T)
    with nc.Block() as b:
        @b.vector
        def _(v):
            entry(v, b2)
            for dt in range(NDT):
                v.tensor_sub(TMPB[:, 0:1024], XNP[:, dt * 1024:(dt + 1) * 1024], MBCB[:, 0:1024])
                chain(v, "v", v.tensor_sub(TMPB[:, 1024:2048], XNO[:, dt * 1024:(dt + 1) * 1024], MBCB[:, 1024:2048]))
                v.tensor_mul(XNP[:, dt * 1024:(dt + 1) * 1024], TMPB[:, 0:1024], RBCB[:, 0:1024])
                v.tensor_mul(XNO[:, dt * 1024:(dt + 1) * 1024], TMPB[:, 1024:2048], RBCB[:, 1024:2048]).then_inc(sem["sVE"], 3)
            v.wait_ge(sem["sCHK"], b2["sCHK"] + 32)
            v.tensor_copy(A01T[0:40, :], gA[0:40, :]).then_inc(sem["sVE"], 1)

        @b.scalar
        def _(sc):
            entry(sc, b2)
            for p in range(32):
                if p >= 3:
                    sc.wait_ge(sem["sCHK"], b2["sCHK"] + (p - 2))
                sc.dma_start(TTS2[p % 3][:],
                             tth[p * 128:(p + 1) * 128, :]).then_inc(sem["sW%d" % (p % 3)], 16)

        @b.tensor
        def _(t):
            entry(t, b2)
            for p in range(32):
                t.wait_ge(sem["sW%d" % (p % 3)], b2["sW%d" % (p % 3)] + 16 * (p // 3 + 1))
                last = None
                for c in range(8):
                    last = t.matmul(gA[0:40, c * 512:(c + 1) * 512], W01S[:, p * 40:(p + 1) * 40],
                                    TTS2[p % 3][:, c * 512:(c + 1) * 512],
                                    start=(p == 0), stop=(p == 31))
                last.then_inc(sem["sCHK"], 1)
    T["sVE"] += 3 * NDT + 1
    T["sW0"] += 16 * 11
    T["sW1"] += 16 * 11
    T["sW2"] += 16 * 10
    T["sCHK"] += 32
    psA_g.__exit__(None, None, None)

    if upto < 3:
        return finish()
    psE_g = nc.psum_tensor("psE", [40, 1024], F32)
    gE = psE_g.__enter__()
    b3a = dict(T)
    with nc.Block() as b:
        @b.tensor
        def _(t):
            entry(t, b3a)
            last = None
            for dt in range(NDT):
                for c in range(2):
                    last = t.matmul(gE[0:40, c * 512:(c + 1) * 512], ENC01[:, dt * 40:(dt + 1) * 40],
                                    XNO[:, dt * 1024 + c * 512:dt * 1024 + (c + 1) * 512],
                                    start=(dt == 0), stop=(dt == NDT - 1))
            last.then_inc(sem["sPEG"], 1)

        @b.scalar
        def _(sc):
            entry(sc, b3a)
            sc.wait_ge(sem["sPEG"], b3a["sPEG"] + 1)
            sc.copy(E01T[0:40, :], gE[0:40, :]).then_inc(sem["sAC"], 1)
    T["sPEG"] += 1
    T["sAC"] += 1
    psE_g.__exit__(None, None, None)

    psK_g = [nc.psum_tensor(f"psK{i}", [128, 1024], F32) for i in range(4)]
    psK = [g.__enter__() for g in psK_g]
    b3 = dict(T)
    with nc.Block() as b:
        @b.gpsimd
        def _(g):
            entry(g, b3)
            g.dma_start(COSD[:], ropec).then_inc(sem["sT2"], 16)
            g.dma_start(SINSGN[:], ropes).then_inc(sem["sT2"], 16)
            wchr = [WCH[0], WCH[1], PB, TMS]
            for i in range(128):
                g2, rp = i // 16, i % 16
                grp, otp = g2 // 2, g2 % 2
                og = grp % 2
                if i >= 4:
                    g.wait_ge(sem["sCHK"], b3["sCHK"] + (i - 3))
                g.dma_start(wchr[i % 4][:, 0:512].rearrange("p (b c) -> p b c", c=256),
                            wk[rp * 256:(rp + 1) * 256,
                               og * 512 + otp * 256:og * 512 + (otp + 1) * 256].rearrange(
                                "(b p) c -> p b c", p=128)).then_inc(sem["sW%d" % (i % 4)], 16)

        @b.tensor
        def _(t):
            entry(t, b3)
            wchr = [WCH[0], WCH[1], PB, TMS]
            for g2 in range(8):
                grp, otp = g2 // 2, g2 % 2
                sh = grp // 2
                pd = (g2 % 2) * 2
                if g2 >= 2:
                    t.wait_ge(sem["sAC"], b3["sAC"] + 2 * (g2 - 1))
                src = XNP if sh == 0 else XNO
                for rp in range(16):
                    i = g2 * 16 + rp
                    t.wait_ge(sem["sW%d" % (i % 4)], b3["sW%d" % (i % 4)] + 16 * (i // 4 + 1))
                    last = None
                    for bb2 in range(2):
                        dt = 2 * rp + bb2
                        for oti in range(2):
                            for c in range(2):
                                last = t.matmul(psK[pd + oti][:, c * 512:(c + 1) * 512],
                                                wchr[i % 4][:, bb2 * 256 + oti * 128:bb2 * 256 + (oti + 1) * 128],
                                                src[:, dt * 1024 + c * 512:dt * 1024 + (c + 1) * 512],
                                                start=(dt == 0), stop=(dt == NDT - 1))
                    last.then_inc(sem["sCHK"], 1)

        @b.scalar
        def _(sc):
            entry(sc, b3)
            for k in range(16):
                g2, oti = k // 2, k % 2
                ps = psK[(g2 % 2) * 2 + oti]
                sc.wait_ge(sem["sCHK"], b3["sCHK"] + 16 * (g2 + 1))
                if k >= 2:
                    sc.wait_ge(sem["sVE"], b3["sVE"] + 3 * (k - 1))
                pbk = PB2 if k % 2 == 0 else PB3
                tmk = TMS2 if k % 2 == 0 else TMS3
                sc.copy(pbk[:], ps[:])
                sc.copy(tmk[0:64, :], ps[64:128, :])
                sc.copy(tmk[64:128, :], ps[0:64, :]).then_inc(sem["sAC"], 1)

        @b.vector
        def _(v):
            entry(v, b3)
            v.wait_ge(sem["sT2"], b3["sT2"] + 32)
            for k in range(16):
                g2, oti = k // 2, k % 2
                grp, otp = g2 // 2, g2 % 2
                sh, og = grp // 2, grp % 2
                h = og * 4 + otp * 2 + oti
                v.wait_ge(sem["sAC"], b3["sAC"] + (k + 1))
                pbk = PB2 if k % 2 == 0 else PB3
                tmk = TMS2 if k % 2 == 0 else TMS3
                v.tensor_mul(PROD[:], pbk[:], COSD[:, sh * 1024:(sh + 1) * 1024])
                chain(v, "v", v.tensor_mul(PROD2[:], tmk[:], SINSGN[:, sh * 1024:(sh + 1) * 1024]))
                v.tensor_add(KT[:, h * 2048 + sh * 1024:h * 2048 + (sh + 1) * 1024],
                             PROD[:], PROD2[:]).then_inc(sem["sVE"], 3)
    T["sT2"] += 32
    for _i in range(4):
        T["sW%d" % _i] += 16 * 32
    T["sCHK"] += 128
    T["sAC"] += 16
    T["sVE"] += 48

    if upto < 4:
        return finish()
    b4 = dict(T)
    with nc.Block() as b:
        @b.gpsimd
        def _(g):
            entry(g, b4)
            wchr = [WCH[0], WCH[1], PB, TMS]
            for i in range(128):
                if i >= 4:
                    g.wait_ge(sem["sCHK"], b4["sCHK"] + (i - 3))
                p4, dt = i // 32, i % 32
                oh = p4 % 2
                g.dma_start(wchr[i % 4][:, 0:512],
                            wv[dt * 128:(dt + 1) * 128, oh * 512:(oh + 1) * 512]).then_inc(sem["sW%d" % (i % 4)], 16)

        @b.tensor
        def _(t):
            entry(t, b4)
            wchr = [WCH[0], WCH[1], PB, TMS]
            for p4 in range(4):
                stg2, oh = p4 // 2, p4 % 2
                if p4 >= 1:
                    t.wait_ge(sem["sVE"], b4["sVE"] + 8 * p4)
                for dt in range(NDT):
                    i = p4 * 32 + dt
                    t.wait_ge(sem["sW%d" % (i % 4)], b4["sW%d" % (i % 4)] + 16 * (i // 4 + 1))
                    last = None
                    for sti in range(8):
                        st = stg2 * 8 + sti
                        src = XNP if st < 8 else XNO
                        col = dt * 1024 + (st % 8) * 128
                        last = t.matmul(psK[sti // 2][:, (sti % 2) * 512:(sti % 2) * 512 + 512],
                                        src[:, col:col + 128],
                                        wchr[i % 4][:, 0:512],
                                        start=(dt == 0), stop=(dt == NDT - 1))
                    last.then_inc(sem["sCHK"], 1)

        @b.vector
        def _(v):
            entry(v, b4)
            bufs = [PB2, TMS2, PB3, TMS3]
            for j in range(32):
                p4, sti = j // 8, j % 8
                v.wait_ge(sem["sCHK"], b4["sCHK"] + 32 * (p4 + 1))
                if j >= 4:
                    v.wait_ge(sem["sSC"], b4["sSC"] + 16 * (j - 3))
                v.tensor_copy(bufs[j % 4][:, 0:512],
                              psK[sti // 2][:, (sti % 2) * 512:(sti % 2) * 512 + 512]).then_inc(sem["sVE"], 1)

        @b.scalar
        def _(sc):
            entry(sc, b4)
            bufs = [PB2, TMS2, PB3, TMS3]
            for j in range(32):
                p4, sti = j // 8, j % 8
                stg2, oh = p4 // 2, p4 % 2
                st = stg2 * 8 + sti
                sc.wait_ge(sem["sVE"], b4["sVE"] + j + 1)
                sc.dma_start(vsp[st * 128:(st + 1) * 128, oh * 512:(oh + 1) * 512],
                             bufs[j % 4][:, 0:512]).then_inc(sem["sSC"], 16)
            sc.wait_ge(sem["sSC"], b4["sSC"] + 16 * 32)
    for _i in range(4):
        T["sW%d" % _i] += 16 * 32
    T["sCHK"] += 128
    T["sVE"] += 32
    T["sSC"] += 16 * 32

    if upto < 5:
        return finish()
    b5 = dict(T)
    with nc.Block() as b:
        @b.gpsimd
        def _(g):
            entry(g, b5)
            wchr = [WCH[0], WCH[1], PB, TMS]
            for i in range(256):
                if i >= 4:
                    g.wait_ge(sem["sCHK"], b5["sCHK"] + (i - 3))
                og2, rp = i // 16, i % 16
                og, otp = og2 // 2, og2 % 2
                g.dma_start(wchr[i % 4][:, 0:512].rearrange("p (b c) -> p b c", c=256),
                            wq[rp * 256:(rp + 1) * 256,
                               og * 512 + otp * 256:og * 512 + (otp + 1) * 256].rearrange(
                                "(b p) c -> p b c", p=128)).then_inc(sem["sW%d" % (i % 4)], 16)

        @b.tensor
        def _(t):
            entry(t, b5)
            wchr = [WCH[0], WCH[1], PB, TMS]
            for og2 in range(16):
                pd = (og2 % 2) * 2
                if og2 >= 2:
                    t.wait_ge(sem["sAC"], b5["sAC"] + 2 * (og2 - 1))
                for rp in range(16):
                    i = og2 * 16 + rp
                    t.wait_ge(sem["sW%d" % (i % 4)], b5["sW%d" % (i % 4)] + 16 * (i // 4 + 1))
                    last = None
                    for bb2 in range(2):
                      dt = 2 * rp + bb2
                      for oti in range(2):
                        for c in range(2):
                            last = t.matmul(psK[pd + oti][:, c * 512:(c + 1) * 512],
                                            wchr[i % 4][:, bb2 * 256 + oti * 128:bb2 * 256 + (oti + 1) * 128],
                                            XNO[:, dt * 1024 + c * 512:dt * 1024 + (c + 1) * 512],
                                            start=(dt == 0), stop=(dt == NDT - 1))
                    last.then_inc(sem["sCHK"], 1)

        @b.scalar
        def _(sc):
            entry(sc, b5)
            for k in range(32):
                og2, oti = k // 2, k % 2
                ps = psK[(og2 % 2) * 2 + oti]
                sc.wait_ge(sem["sCHK"], b5["sCHK"] + 16 * (og2 + 1))
                if k >= 2:
                    sc.wait_ge(sem["sVE"], b5["sVE"] + 3 * (k - 1))
                pbk = PB2 if k % 2 == 0 else PB3
                tmk = TMS2 if k % 2 == 0 else TMS3
                sc.copy(pbk[:], ps[:])
                sc.copy(tmk[0:64, :], ps[64:128, :])
                sc.copy(tmk[64:128, :], ps[0:64, :]).then_inc(sem["sAC"], 1)

        @b.vector
        def _(v):
            entry(v, b5)
            for k in range(32):
                hg = k
                v.wait_ge(sem["sAC"], b5["sAC"] + (k + 1))
                pbk = PB2 if k % 2 == 0 else PB3
                tmk = TMS2 if k % 2 == 0 else TMS3
                v.tensor_mul(PROD[:], pbk[:], COSD[:, 1024:2048])
                chain(v, "v", v.tensor_mul(PROD2[:], tmk[:], SINSGN[:, 1024:2048]))
                v.tensor_add(XNP[:, hg * 1024:(hg + 1) * 1024], PROD[:], PROD2[:]).then_inc(sem["sVE"], 3)
    for _i in range(4):
        T["sW%d" % _i] += 16 * 64
    T["sCHK"] += 256
    T["sAC"] += 32
    T["sVE"] += 96

    if upto < 6:
        return finish()
    b6 = dict(T)
    with nc.Block() as b:
        @b.gpsimd
        def _(g):
            entry(g, b6)
            wchr = [WCH[0], WCH[1], PB, TMS]
            for i in range(256):
                if i >= 4:
                    g.wait_ge(sem["sCHK"], b6["sCHK"] + (i - 3))
                og2, rp = i // 16, i % 16
                og, otp = og2 // 2, og2 % 2
                g.dma_start(wchr[i % 4][:, 0:512].rearrange("p (b c) -> p b c", c=256),
                            wbT[rp * 256:(rp + 1) * 256,
                                og * 512 + otp * 256:og * 512 + (otp + 1) * 256].rearrange(
                                "(b p) c -> p b c", p=128)).then_inc(sem["sW%d" % (i % 4)], 16)

        @b.tensor
        def _(t):
            entry(t, b6)
            wchr = [WCH[0], WCH[1], PB, TMS]
            for og2 in range(16):
                pd = (og2 % 2) * 2
                if og2 >= 2:
                    t.wait_ge(sem["sAC"], b6["sAC"] + 2 * (og2 - 1))
                for rp in range(16):
                    i = og2 * 16 + rp
                    t.wait_ge(sem["sW%d" % (i % 4)], b6["sW%d" % (i % 4)] + 16 * (i // 4 + 1))
                    last = None
                    for bb2 in range(2):
                      dt = 2 * rp + bb2
                      for oti in range(2):
                        for c in range(2):
                            last = t.matmul(psK[pd + oti][:, c * 512:(c + 1) * 512],
                                            wchr[i % 4][:, bb2 * 256 + oti * 128:bb2 * 256 + (oti + 1) * 128],
                                            XNO[:, dt * 1024 + c * 512:dt * 1024 + (c + 1) * 512],
                                            start=(dt == 0), stop=False)
                    last.then_inc(sem["sCHK"], 1)
                for oti in range(2):
                    k = og2 * 2 + oti
                    last = None
                    for c in range(2):
                        last = t.matmul(psK[pd + oti][:, c * 512:(c + 1) * 512],
                                        A01T[0:40, k * 128:(k + 1) * 128],
                                        E01T[0:40, c * 512:(c + 1) * 512],
                                        start=False, stop=True)
                    last.then_inc(sem["sPE"], 1)

        @b.scalar
        def _(sc):
            entry(sc, b6)
            for k in range(32):
                og2, oti = k // 2, k % 2
                ps = psK[(og2 % 2) * 2 + oti]
                sc.wait_ge(sem["sPE"], b6["sPE"] + k + 1)
                if k >= 2:
                    sc.wait_ge(sem["sG%d" % (k % 2)], b6["sG%d" % (k % 2)] + 16 * (k // 2))
                sc.activation(G[k % 2][:], ps[:], AF.Silu, bias=BB[:, k:k + 1]).then_inc(sem["sAC"], 1)
                sc.wait_ge(sem["sAC"], b6["sAC"] + k + 1)
                sc.dma_start(gsp[k * 128:(k + 1) * 128, :], G[k % 2][:]).then_inc(sem["sG%d" % (k % 2)], 16)
            sc.wait_ge(sem["sG0"], b6["sG0"] + 16 * 16)
            sc.wait_ge(sem["sG1"], b6["sG1"] + 16 * 16)
    for _i in range(4):
        T["sW%d" % _i] += 16 * 64
    T["sCHK"] += 256
    T["sPE"] += 32
    T["sAC"] += 32
    T["sG0"] += 16 * 16
    T["sG1"] += 16 * 16
    for g in reversed(psK_g):
        g.__exit__(None, None, None)

    if upto < 7:
        return finish()
    scp_g = [nc.psum_tensor(f"scp{i}", [128, 1024], F32) for i in range(2)]
    scp = [g.__enter__() for g in scp_g]
    sums_g = [nc.psum_tensor("sums0", [128, 512], F32)]
    sums_ps = [g.__enter__() for g in sums_g]
    av_g = [nc.psum_tensor(f"av{i}", [128, 512], F32) for i in range(3)]
    av_ps = [g.__enter__() for g in av_g]
    units = [(h, g, qc) for h in range(8) for g in range(4) for qc in range(2)]
    units = units[:nunits]
    NU = len(units)
    b7 = dict(T)

    def attn_pair(seti, jp):
        col = 16384 + (seti * 12 + 2 * jp) * 512
        return XNO[:, col:col + 1024]

    def exptmp_pair(i):
        col = 16384 + 24 * 512 + i * 1024
        return XNO[:, col:col + 1024]

    ORD = [0, 1, 4, 5, 2, 3]

    def qlo(j):
        return 128 * (j - 8) if j >= 8 else 0

    def qhi(j):
        return min(512, 128 * (j + 1)) if j < 4 else 512

    def unit_scores(t, u):
        h, g_, qc = units[u]
        hg = h * 4 + g_
        for pos, jp in enumerate(ORD):
            tgt = 6 * u + pos - 1
            if tgt > 0:
                t.wait_ge(sem["sEXP"], b7["sEXP"] + tgt)
            for jj in range(2):
                j = 2 * jp + jj
                st = 4 * qc + j
                lo, hi = qlo(j), qhi(j)
                t.matmul(scp[pos % 2][:, jj * 512 + lo:jj * 512 + hi],
                         KT[:, h * 2048 + st * 128:h * 2048 + (st + 1) * 128],
                         XNP[:, hg * 1024 + qc * 512 + lo:hg * 1024 + qc * 512 + hi],
                         start=True, stop=True).then_inc(sem["sSCR"], 1)

    with nc.Block() as b:
        @b.sync
        def _(s):
            entry(s, b7)
            for st in range(16):
                s.dma_start(XNO[:, st * 1024:(st + 1) * 1024],
                            vsp[st * 128:(st + 1) * 128, :]).then_inc(sem["sIN"], 16)

        @b.gpsimd
        def _(g):
            entry(g, b7)
            for j in range(8):
                g.dma_start(MASKS[:, j * 512:(j + 1) * 512],
                            masks8[j * 128:(j + 1) * 128, :]).then_inc(sem["sGP"], 16)
            g.dma_start(CORRS[:], corr).then_inc(sem["sGP"], 16)

        @b.tensor
        def _(t):
            entry(t, b7)
            if NU > 0:
                unit_scores(t, 0)
            for u in range(NU):
                h, g_, qc = units[u]
                if u + 1 < NU:
                    unit_scores(t, u + 1)
                t.wait_ge(sem["sEXP"], b7["sEXP"] + 6 * u + 6)
                t.wait_ge(sem["sMS"], b7["sMS"] + 4 * u + 4)
                if u >= 1:
                    t.wait_ge(sem["sCB"], b7["sCB"] + u)
                last = None
                for j in range(12):
                    lo, hi = qlo(j), qhi(j)
                    last = t.matmul(sums_ps[0][:, lo:hi], ONESBF[:],
                                    attn_tile(u % 2, j)[:, lo:hi],
                                    start=(j == 0), stop=(j == 11))
                last.then_inc(sem["sSUM"], 1)
                if u == 0:
                    t.wait_ge(sem["sIN"], b7["sIN"] + 16 * 16)
                if u >= 3:
                    t.wait_ge(sem["sOE"], b7["sOE"] + u - 2)
                last = None
                for j in range(12):
                    st = 4 * qc + j
                    lo, hi = qlo(j), qhi(j)
                    last = t.matmul(av_ps[u % 3][:, lo:hi],
                                    XNO[:, st * 1024 + h * 128:st * 1024 + (h + 1) * 128],
                                    attn_tile(u % 2, j)[:, lo:hi], start=(j == 0), stop=(j == 11))
                last.then_inc(sem["sAV"], 1)

        @b.scalar
        def _(sc):
            entry(sc, b7)
            def recip(u):
                sc.wait_ge(sem["sCB"], b7["sCB"] + u + 1)
                if u >= 2:
                    sc.wait_ge(sem["sOE"], b7["sOE"] + u - 1)
                chain(sc, "s", sc.activation(LNT[:], SUMS[u % 2][:], AF.Ln))
                sc.activation(RECIP[u % 2][:], LNT[:], AF.Exp, scale=-1.0).then_inc(sem["sRC"], 1)
            for u in range(NU):
                for pos, jp in enumerate(ORD):
                    sc.wait_ge(sem["sSCR"], b7["sSCR"] + 12 * u + 2 * pos + 2)
                    if pos == 0 and u >= 2:
                        sc.wait_ge(sem["sAV"], b7["sAV"] + u - 1)
                    if jp < 2:
                        tgt = 4 * u + jp - 1
                        if tgt > 0:
                            sc.wait_ge(sem["sMS"], b7["sMS"] + tgt)
                        dst = exptmp_pair(jp)
                    elif jp >= 4:
                        sc.wait_ge(sem["sMS"], b7["sMS"] + 4 * u + jp - 3)
                        dst = exptmp_pair(jp - 4)
                    else:
                        dst = attn_pair(u % 2, jp)
                    sc.activation(dst, scp[pos % 2][:], AF.Exp, scale=SM).then_inc(sem["sEXP"], 1)
                if u >= 1:
                    recip(u - 1)
            if NU > 0:
                recip(NU - 1)

        @b.vector
        def _(v):
            entry(v, b7)
            v.wait_ge(sem["sGP"], b7["sGP"] + 16 * 9)

            def vmasks(u):
                for mp in range(4):
                    jp = mp if mp < 2 else mp + 2
                    v.wait_ge(sem["sEXP"], b7["sEXP"] + 6 * u + mp + 1)
                    if mp == 0 and u >= 2:
                        v.wait_ge(sem["sAV"], b7["sAV"] + u - 1)
                    v.tensor_mul(attn_pair(u % 2, jp), exptmp_pair(jp % 2 if jp < 2 else jp - 4),
                                 MASKS[:, mp * 1024:(mp + 1) * 1024]).then_inc(sem["sMS"], 1)

            if NU > 0:
                vmasks(0)
            for u in range(NU):
                h, g_, qc = units[u]
                if u + 1 < NU:
                    vmasks(u + 1)
                v.wait_ge(sem["sSUM"], b7["sSUM"] + u + 1)
                if u >= 2:
                    v.wait_ge(sem["sRC"], b7["sRC"] + u - 1)
                v.tensor_sub(SUMS[u % 2][:], sums_ps[0][:],
                             CORRS[:, qc * 512:(qc + 1) * 512]).then_inc(sem["sCB"], 1)
                v.wait_ge(sem["sAV"], b7["sAV"] + u + 1)
                v.wait_ge(sem["sRC"], b7["sRC"] + u + 1)
                hg = h * 4 + g_
                v.tensor_mul(XNP[:, hg * 1024 + qc * 512:hg * 1024 + (qc + 1) * 512],
                             av_ps[u % 3][:], RECIP[u % 2][:]).then_inc(sem["sOE"], 1)
    T["sIN"] += 16 * 16
    T["sGP"] += 16 * 9
    T["sSCR"] += 12 * NU
    T["sEXP"] += 6 * NU
    T["sMS"] += 4 * NU
    T["sSUM"] += NU
    T["sAV"] += NU
    T["sCB"] += NU
    T["sRC"] += NU
    T["sOE"] += NU
    for g in reversed(scp_g + sums_g + av_g):
        g.__exit__(None, None, None)

    if upto < 8:
        return finish()
    psO_g = [nc.psum_tensor(f"psO{i}", [128, 1024], F32) for i in range(4)]
    psO = [g.__enter__() for g in psO_g]
    b8 = dict(T)
    with nc.Block() as b:
        @b.gpsimd
        def _(g):
            entry(g, b8)
            wchr = [WCH[0], WCH[1], PB, TMS]
            for i in range(256):
                if i >= 4:
                    g.wait_ge(sem["sCHK"], b8["sCHK"] + (i - 3))
                og2, rp = i // 16, i % 16
                og, otp = og2 // 2, og2 % 2
                g.dma_start(wchr[i % 4][:, 0:512].rearrange("p (b c) -> p b c", c=256),
                            wo[rp * 256:(rp + 1) * 256,
                               og * 512 + otp * 256:og * 512 + (otp + 1) * 256].rearrange(
                                "(b p) c -> p b c", p=128)).then_inc(sem["sW%d" % (i % 4)], 16)

        @b.scalar
        def _(sc):
            entry(sc, b8)
            for k in range(32):
                if k >= 2:
                    sc.wait_ge(sem["sVE"], b8["sVE"] + 2 * (k - 2) + 2)
                sc.dma_start(GTB[k % 2][:], gsp[k * 128:(k + 1) * 128, :]).then_inc(sem["sP%d" % (k % 2)], 16)
                sc.dma_start(XTT[k % 2][:], xT[k * 128:(k + 1) * 128, 1024:2048]).then_inc(sem["sP%d" % (k % 2)], 16)

        @b.tensor
        def _(t):
            entry(t, b8)
            wchr = [WCH[0], WCH[1], PB, TMS]
            for og2 in range(16):
                pd = (og2 % 2) * 2
                if og2 >= 2:
                    t.wait_ge(sem["sVE"], b8["sVE"] + 4 * (og2 - 1))
                for rp in range(16):
                    i = og2 * 16 + rp
                    t.wait_ge(sem["sW%d" % (i % 4)], b8["sW%d" % (i % 4)] + 16 * (i // 4 + 1))
                    last = None
                    for bb2 in range(2):
                      ht = 2 * rp + bb2
                      for oti in range(2):
                        for c in range(2):
                            last = t.matmul(psO[pd + oti][:, c * 512:(c + 1) * 512],
                                            wchr[i % 4][:, bb2 * 256 + oti * 128:bb2 * 256 + (oti + 1) * 128],
                                            XNP[:, ht * 1024 + c * 512:ht * 1024 + (c + 1) * 512],
                                            start=(ht == 0), stop=(ht == NDT - 1))
                    last.then_inc(sem["sCHK"], 1)

        @b.vector
        def _(v):
            entry(v, b8)
            for k in range(32):
                og2, oti = k // 2, k % 2
                v.wait_ge(sem["sCHK"], b8["sCHK"] + 16 * (og2 + 1))
                v.wait_ge(sem["sP%d" % (k % 2)], b8["sP%d" % (k % 2)] + 32 * (k // 2 + 1))
                if k >= 4:
                    v.wait_ge(sem["sG%d" % (k % 2)], b8["sG%d" % (k % 2)] + 16 * (k // 2 - 1))
                pso = psO[(og2 % 2) * 2 + oti]
                chain(v, "v", v.tensor_mul(OUTA[k % 2][:], pso[:], GTB[k % 2][:]))
                v.tensor_add(OUTB[k % 4][:], OUTA[k % 2][:], XTT[k % 2][:]).then_inc(sem["sVE"], 2)

        @b.sync
        def _(s):
            entry(s, b8)
            for k in range(32):
                s.wait_ge(sem["sVE"], b8["sVE"] + 2 * (k + 1))
                s.dma_start(out_t[k * 128:(k + 1) * 128, :], OUTB[k % 4][:]).then_inc(sem["sG%d" % (k % 2)], 16)
            s.wait_ge(sem["sG0"], b8["sG0"] + 16 * 16)
            s.wait_ge(sem["sG1"], b8["sG1"] + 16 * 16)
    for _i in range(4):
        T["sW%d" % _i] += 16 * 64
    T["sCHK"] += 256
    T["sP0"] += 16 * 32
    T["sP1"] += 16 * 32
    T["sVE"] += 64
    T["sG0"] += 16 * 16
    T["sG1"] += 16 * 16
    for g in reversed(psO_g):
        g.__exit__(None, None, None)

    return finish()


_CACHE = {}


def prep_in_maps(L0_attn_x, L0_attn_wq, L0_attn_wk, L0_attn_wv, L0_attn_wo,
                 L0_ffn_W_base, L0_ffn_b_base, L0_ffn_enc0, L0_ffn_enc1,
                 L0_ffn_coefs0, L0_ffn_coefs1, L0_ffn_mask0, L0_ffn_mask1):
    x2d = np.asarray(L0_attn_x, dtype=np.float32).reshape(S, D)

    wq = np.ascontiguousarray(np.asarray(L0_attn_wq, np.float32))
    wk = np.ascontiguousarray(np.asarray(L0_attn_wk, np.float32))
    wv = np.ascontiguousarray(np.asarray(L0_attn_wv, np.float32))
    wo = np.ascontiguousarray(np.asarray(L0_attn_wo, np.float32))
    wbT = np.ascontiguousarray(np.asarray(L0_ffn_W_base, np.float32).T)
    bbt = np.ascontiguousarray(np.asarray(L0_ffn_b_base, np.float32).reshape(32, 128).T)
    enc01T = np.ascontiguousarray(np.concatenate(
        [np.asarray(L0_ffn_enc0, np.float32), np.asarray(L0_ffn_enc1, np.float32)], axis=0).T)
    c0 = np.asarray(L0_ffn_coefs0, np.float32)
    c1 = np.asarray(L0_ffn_coefs1, np.float32)
    m0 = np.asarray(L0_ffn_mask0).astype(np.uint8)
    m1 = np.asarray(L0_ffn_mask1).astype(np.uint8)
    coefs01 = np.ascontiguousarray(np.concatenate(
        [c0[0], c1[0], c0[1], c1[1], c0[2], c1[2]], axis=1))
    mask01 = np.ascontiguousarray(np.concatenate(
        [m0[0], m1[0], m0[1], m1[1], m0[2], m1[2]], axis=1))

    tth = _tth()
    masks8 = _masks8()

    in_maps = []
    for core in range(NCORE):
        lo = core * OWN
        xTc = np.zeros((SLOC, D), np.float32)
        s0 = max(0, lo - 1024)
        xTc[SLOC - (lo + 1024 - s0):] = x2d[s0:lo + 1024]
        xTc = np.ascontiguousarray(xTc.T)
        cosd, sinsgn = _rope_tables(core)
        in_maps.append({
            "xT": xTc, "wq": wq, "wk": wk, "wv": wv, "wo": wo, "wbT": wbT,
            "bbt": bbt, "enc01T": enc01T, "coefs01": coefs01, "mask01": mask01,
            "tth": tth, "ropec": cosd, "ropes": sinsgn, "masks8": masks8,
            "corr": _corr(core),
        })
    return in_maps


def kernel(**inputs):
    in_maps = prep_in_maps(**inputs)
    if "nc" not in _CACHE:
        _CACHE["nc"] = build_nc()
    nc, _stack = _CACHE["nc"]
    res = run_bass_kernel_spmd(nc, in_maps, core_ids=list(range(NCORE)))
    out = np.concatenate([np.ascontiguousarray(r["outT"].T) for r in res.results], axis=0)
    return out.reshape(1, S, D)
